# version 02
# baseline (speedup 1.0000x reference)
import contextlib
import numpy as np
import ml_dtypes
import concourse.bass as bass
import concourse.mybir as mybir
from concourse.bass_utils import run_bass_kernel_spmd
from concourse.bass import IndirectOffsetOnAxis

F32 = mybir.dt.float32
BF16 = mybir.dt.bfloat16
I32 = mybir.dt.int32
AF = mybir.ActivationFunctionType
ALU = mybir.AluOpType
AX = mybir.AxisListType
BF = ml_dtypes.bfloat16

B, S, D = 4, 4096, 2048
NCH = 16
TOK = 2048
NH_A = 16
QD_A, NOPE_A, ROPE_A, VD_A = 192, 128, 64, 128
LORA = 512
SEL, OWNSEL = 512, 256
IFF, NICH = 8192, 64
EPS_LN, EPS_RMS = 1e-5, 1e-6

R_H = {0: [(0, 1024), (3072, 4096)], 1: [(1024, 3072)]}
OWN_POS = {h: np.concatenate([np.arange(a, b) for a, b in R_H[h]]) for h in (0, 1)}
STORAGE_TO_TRUE = np.concatenate([OWN_POS[0], OWN_POS[1]])
TRUE_TO_STORAGE = np.argsort(STORAGE_TO_TRUE).astype(np.int64)
S2T_RANGES = [(0, 1024, 0), (1024, 2048, 2048), (3072, 1024, 1024)]

DEBUG_TAPS = []
PHASES = 4


def _fm(a2d):
    Rr, C = a2d.shape
    return np.ascontiguousarray(a2d.reshape(Rr // 128, 128, C).transpose(1, 0, 2).reshape(128, -1))


def _col128(vec):
    return np.ascontiguousarray(np.asarray(vec).reshape(-1, 128).T)


def _rope_halves(d_half, positions):
    d = 2 * d_half
    inv = 1.0 / (10000.0 ** (np.arange(0, d, 2, dtype=np.float64) / d))
    ang = positions[None, :].astype(np.float64) * inv[:, None]
    return np.cos(ang).astype(np.float32), np.sin(ang).astype(np.float32)


def host_prepare(inp):
    x = np.asarray(inp["L0_attn_a_hidden_states"])
    Wq = np.asarray(inp["L0_attn_a_Wq"])
    wq_a = np.stack([_fm(Wq[:, h * QD_A:(h + 1) * QD_A]) for h in range(NH_A)])
    wkv_a = _fm(np.asarray(inp["L0_attn_a_Wkv_down"]))
    Wk_up = np.asarray(inp["L0_attn_a_Wk_up"])
    wk_up_a = np.stack([_fm(Wk_up[:, h * QD_A:(h + 1) * QD_A]) for h in range(NH_A)])
    Wv_up = np.asarray(inp["L0_attn_a_Wv_up"])
    wv_up_a = np.stack([_fm(Wv_up[:, h * VD_A:(h + 1) * VD_A]) for h in range(NH_A)])
    Wo = np.asarray(inp["L0_attn_a_Wo"])
    wo_a = np.stack([_fm(Wo[512 * g:512 * (g + 1), :]) for g in range(4)])

    router_w = _col128(inp["L0_ffn_w_router"])
    ln1w = _col128(inp["L0_ffn_ln1_w"])
    ln2w = _col128(inp["L0_ffn_ln2_w"])
    Wqf = np.asarray(inp["L0_ffn_wq"])
    wq_f = np.stack([_fm(Wqf[:, m * 128:(m + 1) * 128]) for m in range(16)])
    Wkf = np.asarray(inp["L0_ffn_wk"])
    wk_f = np.stack([_fm(Wkf[:, m * 128:(m + 1) * 128]) for m in range(8)])
    Wvf = np.asarray(inp["L0_ffn_wv"])
    wv_f = np.stack([_fm(Wvf[:, m * 128:(m + 1) * 128]) for m in range(8)])
    bq_f = _col128(inp["L0_ffn_bq"]); bk_f = _col128(inp["L0_ffn_bk"]); bv_f = _col128(inp["L0_ffn_bv"])
    Wof = np.asarray(inp["L0_ffn_wo"])
    wo_f = np.stack([np.ascontiguousarray(
        Wof[:, dc * 128:(dc + 1) * 128].reshape(16, 128, 128).transpose(1, 0, 2).reshape(128, 2048))
        for dc in range(16)])
    Wg = np.asarray(inp["L0_ffn_w_gate"])
    wg_f = np.stack([_fm(Wg[:, i * 128:(i + 1) * 128]) for i in range(NICH)])
    Wu = np.asarray(inp["L0_ffn_w_up"])
    wu_f = np.stack([_fm(Wu[:, i * 128:(i + 1) * 128]) for i in range(NICH)])
    Wd = np.asarray(inp["L0_ffn_w_down"])
    wd_f = np.stack([np.ascontiguousarray(Wd[i * 128:(i + 1) * 128, :]) for i in range(NICH)])

    w_attn = np.asarray(inp["L0_attn_b_w_attn"])
    b_attn = np.asarray(inp["L0_attn_b_b_attn"])
    w_proj = np.asarray(inp["L0_attn_b_w_proj"])
    b_proj = np.asarray(inp["L0_attn_b_b_proj"])

    st_idx = np.ascontiguousarray(TRUE_TO_STORAGE.astype(np.float32).reshape(128, 32))
    triw = (np.arange(128)[:, None] < np.arange(128)[None, :]).astype(np.float32)
    trimask = np.zeros((128, 4, 512), np.float32)
    for o in range(4):
        trimask[:, o, :] = np.where(128 * o + np.arange(128)[:, None] <= np.arange(512)[None, :],
                                    0.0, -1e6)
    trimask = trimask.reshape(128, 4 * 512)
    cosf, sinf = _rope_halves(64, np.arange(S))
    cos_f_tbl = np.concatenate([np.concatenate([cosf, cosf], 0).T,
                                np.concatenate([sinf, sinf], 0).T], axis=1)

    maps = []
    for c in range(8):
        b, h = c // 2, c % 2
        own = OWN_POS[h]
        xin = np.ascontiguousarray(x[b][own].T.reshape(NCH, 128, TOK)).astype(np.float32)
        cq, sq = _rope_halves(32, own)
        ropeq_cs = np.concatenate([cq, sq], axis=0)
        ropeq_sc = np.concatenate([sq, cq], axis=0)
        ck, sk = _rope_halves(32, STORAGE_TO_TRUE)
        ropek_cs = np.concatenate([ck, sk], axis=0)
        ropek_sc = np.concatenate([sk, ck], axis=0)
        qflat = (512 * b + 256 * h + np.arange(OWNSEL)).astype(np.float32)[None, :]
        jownv = np.broadcast_to((256 * h + 1 + np.arange(OWNSEL)).astype(np.float32)[None, :],
                                (128, OWNSEL)).copy()
        ownposv = own.astype(np.float32)[None, :]
        bpmv = _col128(b_proj)
        wqb8 = np.stack([_fm(w_attn[:, 128 * (8 * h + m):128 * (8 * h + m) + 128]) for m in range(8)])
        wkb8 = np.stack([_fm(w_attn[:, D + 128 * (8 * h + m):D + 128 * (8 * h + m) + 128]) for m in range(8)])
        wvb8 = np.stack([_fm(w_attn[:, 2 * D + 128 * (8 * h + m):2 * D + 128 * (8 * h + m) + 128]) for m in range(8)])
        bqb8 = np.ascontiguousarray(b_attn[0:D].reshape(16, 128).T[:, 8 * h:8 * h + 8])
        bkb8 = np.ascontiguousarray(b_attn[D:2 * D].reshape(16, 128).T[:, 8 * h:8 * h + 8])
        bvb8 = np.ascontiguousarray(b_attn[2 * D:3 * D].reshape(16, 128).T[:, 8 * h:8 * h + 8])
        wpb8 = np.stack([np.ascontiguousarray(w_proj[128 * (8 * h + m):128 * (8 * h + m) + 128, :])
                         for m in range(8)])
        maps.append(dict(
            xin=xin,
            wq_a=wq_a, wkv_a=wkv_a, wk_up_a=wk_up_a, wv_up_a=wv_up_a, wo_a=wo_a,
            router_w=router_w, ln1w=ln1w, ln2w=ln2w,
            wq_f=wq_f, wk_f=wk_f, wv_f=wv_f, bq_f=bq_f, bk_f=bk_f, bv_f=bv_f,
            wo_f=wo_f, wg_f=wg_f, wu_f=wu_f, wd_f=wd_f,
            wqb8=wqb8, wkb8=wkb8, wvb8=wvb8, bqb8=bqb8, bkb8=bkb8, bvb8=bvb8,
            wpb8=wpb8, bpm=bpmv,
            ropeq_cs=ropeq_cs.astype(BF), ropeq_sc=ropeq_sc.astype(BF),
            ropek_cs=ropek_cs.astype(BF), ropek_sc=ropek_sc.astype(BF),
            trimask=trimask.astype(BF), st_idx=st_idx, triw=triw.astype(BF),
            qflat=qflat, jown=jownv, ownpos=ownposv,
            cos_f_tbl=cos_f_tbl.astype(BF),
        ))
    return maps



class Sem:
    def __init__(self, ctx, nc, name):
        self.h = ctx.enter_context(nc.semaphore(name))
        self.n = 0

    def inc(self, inst, k=1):
        inst.then_inc(self.h, k)
        self.n += k
        return inst

    def dma(self, inst):
        return self.inc(inst, 16)

    def wait(self, eng, n=None):
        eng.wait_ge(self.h, self.n if n is None else n)


class K:

    def __init__(self, taps):
        self.taps = set(taps)
        self.nc = bass.Bass("TRN2", target_bir_lowering=False, debug=False)
        self.tap_t = {}
        self.ctx = contextlib.ExitStack()
        self._tap_bar = 0
        self._bar_seen = {}

    def _wrap(self, fn, eng_name):
        bar = getattr(self, "_tap_bar", 0)
        if bar and self._bar_seen.get(eng_name, 0) < bar:
            self._bar_seen[eng_name] = bar
            ST = self.sems["ST"]

            def g(eng, fn=fn, bar=bar, ST=ST):
                eng.wait_ge(ST.h, bar)
                fn(eng)
            return g
        return fn

    def sync(self, fn): self.blk.sync(self._wrap(fn, "sync"))
    def gp(self, fn): self.blk.gpsimd(self._wrap(fn, "gp"))
    def pe(self, fn): self.blk.tensor(self._wrap(fn, "pe"))
    def dv(self, fn): self.blk.vector(self._wrap(fn, "dv"))
    def ac(self, fn): self.blk.scalar(self._wrap(fn, "ac"))

    def sb(self, name, shape, dt):
        return self.ctx.enter_context(self.nc.sbuf_tensor(name, list(shape), dt))

    def fence(self, eng):
        for s in self.sems.values():
            eng.wait_ge(s.h, s.n)

    def dvs(self, v, inst):
        DV = self.sems["DV"]
        DV.inc(inst)
        v.wait_ge(DV.h, DV.n)
        return inst

    def tap(self, name, src_ap, shape, dt):
        if name not in self.taps:
            return
        t = self.nc.dram_tensor("tap_" + name, list(shape), dt, kind="ExternalOutput").ap()
        self.tap_t[name] = t
        ST = self.sems["ST"]

        def emit(g):
            for s in self.sems.values():
                if s is not ST:
                    g.wait_ge(s.h, s.n)
            ST.dma(g.dma_start(out=t[tuple(slice(0, sh) for sh in shape)], in_=src_ap))
            ST.wait(g)
        self.gp(emit)
        self._tap_bar = self.sems["ST"].n


def mm_groups(total, gmax=512):
    out, o = [], 0
    while o < total:
        w = min(gmax, total - o)
        out.append((o, w))
        o += w
    return out


_STATS_SEQ = [0]


def alloc_stats(k, stack, scores, xstack=None):
    nc = k.nc
    _STATS_SEQ[0] += 1
    sq = _STATS_SEQ[0]
    k.XLN = (xstack if xstack is not None else stack).enter_context(
        nc.sbuf_tensor(f"XLN{sq}", [128, NCH * TOK], BF16,
                       side="right" if xstack is not None else "left"))
    k.STM = stack.enter_context(nc.sbuf_tensor(f"STM{sq}", [1, TOK], F32))
    k.STR = stack.enter_context(nc.sbuf_tensor(f"STR{sq}", [1, TOK], F32))
    k.STROW = stack.enter_context(nc.sbuf_tensor(f"STROW{sq}", [1, TOK], F32))
    k.STROW2 = stack.enter_context(nc.sbuf_tensor(f"STROW2{sq}", [1, TOK], F32))
    k.STBF = stack.enter_context(nc.sbuf_tensor(f"STBF{sq}", [1, TOK], BF16))
    k.MB = stack.enter_context(nc.sbuf_tensor(f"MB{sq}", [128, TOK], BF16))
    k.RB = stack.enter_context(nc.sbuf_tensor(f"RB{sq}", [128, TOK], BF16))
    if scores:
        k.SCRO = stack.enter_context(nc.sbuf_tensor(f"SCRO{sq}", [1, TOK], F32))



def ln_phase(k, io, src, copy_to, exact_scores, eps=EPS_LN, wait_first=()):
    nc = k.nc
    LD, GP, PE, DV, AC = (k.sems[n] for n in ("LD", "GP", "PE", "DV", "AC"))

    _STATS_SEQ[0] += 1
    sq = _STATS_SEQ[0]
    with contextlib.ExitStack() as ph:
        sbu = lambda name, shape, dt: ph.enter_context(nc.sbuf_tensor(f"{name}_{sq}", list(shape), dt))

        xstg = [sbu(f"ln_xstg{i}", [128, TOK], F32) for i in range(2)]
        x2b = [sbu(f"ln_x2b{i}", [128, TOK], BF16) for i in range(2)]
        if exact_scores:
            x2stg = [sbu(f"ln_x2s{i}", [128, TOK], F32) for i in range(2)]
            xlo = [sbu(f"ln_xlo{i}", [128, TOK], BF16) for i in range(2)]
            x2lo = [sbu(f"ln_x2lo{i}", [128, TOK], BF16) for i in range(2)]
        lA = sbu("ln_lA", [128, 33], BF16)
        lB = sbu("ln_lB", [128, 33], BF16)
        if exact_scores:
            lCs = sbu("ln_lCs", [128, NCH * 33], BF16)
            lDs = sbu("ln_lDs", [128, NCH * 33], BF16)
            rwf = sbu("ln_rwf", [128, NCH], F32)
            rlo = sbu("ln_rlo", [128, NCH], F32)
            rwh = sbu("ln_rwh", [128, NCH], BF16)
            rwl = sbu("ln_rwl", [128, NCH], BF16)
            rsum = sbu("ln_rsum", [128, 1], F32)
            rsbh = sbu("ln_rsbh", [128, 1], BF16)
            rsbl = sbu("ln_rsbl", [128, 1], BF16)
            rsum1 = sbu("ln_rsum1", [1, 1], F32)
            onec32 = sbu("ln_onec32", [128, 1], F32)

        def setup_sync(e):
            for s, n in wait_first:
                e.wait_ge(s.h, n)
            if exact_scores:
                LD.dma(e.dma_start(out=rwf[:, :], in_=io["router_w"][:, :]))
        k.sync(setup_sync)

        def setup_dv(v):
            for s, n in wait_first:
                v.wait_ge(s.h, n)
            LD.wait(v)
            v.memset(lA[:, :], 0.0)
            v.memset(lB[:, :], 0.0)
            v.tensor_copy(lA[:, 0:1], k.ONEC[:, :])
            last = v.tensor_copy(lB[:, 32:33], k.ONEC[:, :])
            if exact_scores:
                v.memset(lCs[:, :], 0.0)
                v.memset(lDs[:, :], 0.0)
                v.memset(onec32[:, :], 1.0)
                k.dvs(v, v.tensor_copy(rwh[:, :], rwf[:, :]))
                k.dvs(v, v.tensor_sub(rlo[:, :], rwf[:, :], rwh[:, :]))
                k.dvs(v, v.tensor_copy(rwl[:, :], rlo[:, :]))
                for dc in range(NCH):
                    v.tensor_copy(lCs[:, 33 * dc:33 * dc + 1], rwh[:, dc:dc + 1])
                    v.tensor_copy(lCs[:, 33 * dc + 32:33 * dc + 33], rwl[:, dc:dc + 1])
                    last = v.tensor_copy(lDs[:, 33 * dc:33 * dc + 1], rwh[:, dc:dc + 1])
            DV.inc(last)
        k.dv(setup_dv)

        if exact_scores:
            def sw_dv(v):
                k.dvs(v, v.tensor_add(rlo[:, :], rwh[:, :], rwl[:, :]))
                k.dvs(v, v.reduce_sum(rsum[:, :], rlo[:, :], axis=AX.X))
                k.dvs(v, v.tensor_copy(rsbh[:, :], rsum[:, :]))
                k.dvs(v, v.tensor_sub(rsum[:, :], rsum[:, :], rsbh[:, :]))
                DV.inc(v.tensor_copy(rsbl[:, :], rsum[:, :]))
            k.dv(sw_dv)
            with nc.psum_tensor(f"ps_sw_{sq}", [1, 1], F32) as ps_sw:
                def sw_pe(t):
                    DV.wait(t)
                    t.matmul(ps_sw[0:1, 0:1], rsbh[:, :], k.ONEC[:, :], start=True, stop=False)
                    PE.inc(t.matmul(ps_sw[0:1, 0:1], rsbl[:, :], k.ONEC[:, :], start=False, stop=True))
                k.pe(sw_pe)

                def sw_dv2(v):
                    PE.wait(v)
                    DV.inc(v.tensor_copy(rsum1[0:1, 0:1], ps_sw[0:1, 0:1]))
                k.dv(sw_dv2)
        dv_setup = DV.n

        with contextlib.ExitStack() as pst:
            ps_stat = pst.enter_context(nc.psum_tensor(f"ps_stat_{sq}", [33, TOK], F32))
            ps_sc = pst.enter_context(nc.psum_tensor(f"ps_sc_{sq}", [33, TOK], F32)) if exact_scores else None

            dv_done, ac_done, pe_done, ld_after = {}, {}, {}, {}
            for dc in range(NCH):
                def p_sync(e, dc=dc):
                    if dc == 0:
                        for s, n in wait_first:
                            e.wait_ge(s.h, n)
                    if dc >= 2:
                        e.wait_ge(DV.h, dv_done[dc - 2])
                        e.wait_ge(AC.h, ac_done[dc - 2])
                        e.wait_ge(PE.h, pe_done[dc - 2])
                    LD.dma(e.dma_start(out=xstg[dc % 2][:, :], in_=src[dc, :, :]))
                    ld_after[dc] = LD.n
                k.sync(p_sync)

                if copy_to is not None:
                    def p_gp(g, dc=dc):
                        if dc == 0:
                            for s, n in wait_first:
                                g.wait_ge(s.h, n)
                        GP.dma(g.dma_start(out=copy_to[dc, :, :], in_=src[dc, :, :]))
                    k.gp(p_gp)

                def p_ac(s, dc=dc):
                    s.wait_ge(LD.h, ld_after[dc])
                    dst = x2stg[dc % 2] if exact_scores else x2b[dc % 2]
                    AC.inc(s.activation(dst[:, :], xstg[dc % 2][:, :], AF.Square))
                    ac_done[dc] = AC.n
                k.ac(p_ac)

                def p_dv(v, dc=dc):
                    v.wait_ge(LD.h, ld_after[dc])
                    if dc == 0:
                        v.wait_ge(DV.h, dv_setup)
                    sl = slice(dc * TOK, (dc + 1) * TOK)
                    if exact_scores:
                        v.tensor_copy(k.XLN[:, sl], xstg[dc % 2][:, :])
                        v.tensor_sub(xlo[dc % 2][:, :], xstg[dc % 2][:, :], k.XLN[:, sl])
                        v.wait_ge(AC.h, ac_done[dc])
                        v.tensor_copy(x2b[dc % 2][:, :], x2stg[dc % 2][:, :])
                        DV.inc(v.tensor_sub(x2lo[dc % 2][:, :], x2stg[dc % 2][:, :], x2b[dc % 2][:, :]))
                    else:
                        DV.inc(v.tensor_copy(k.XLN[:, sl], xstg[dc % 2][:, :]))
                    dv_done[dc] = DV.n
                k.dv(p_dv)

                def p_pe(t, dc=dc):
                    t.wait_ge(DV.h, dv_done[dc])
                    t.wait_ge(AC.h, ac_done[dc])
                    sl0 = dc * TOK
                    first, final = (dc == 0), (dc == NCH - 1)
                    mmv = None
                    for (o, w) in mm_groups(TOK):
                        t.matmul(ps_stat[:, o:o + w], lA[:, :], k.XLN[:, sl0 + o:sl0 + o + w],
                                 start=first, stop=False)
                        if exact_scores:
                            t.matmul(ps_stat[:, o:o + w], lA[:, :], xlo[dc % 2][:, o:o + w],
                                     start=False, stop=False)
                            t.matmul(ps_stat[:, o:o + w], lB[:, :], x2b[dc % 2][:, o:o + w],
                                     start=False, stop=False)
                            mmv = t.matmul(ps_stat[:, o:o + w], lB[:, :], x2lo[dc % 2][:, o:o + w],
                                           start=False, stop=final)
                            t.matmul(ps_sc[:, o:o + w], lCs[:, 33 * dc:33 * (dc + 1)],
                                     k.XLN[:, sl0 + o:sl0 + o + w], start=first, stop=False)
                            mmv = t.matmul(ps_sc[:, o:o + w], lDs[:, 33 * dc:33 * (dc + 1)],
                                           xlo[dc % 2][:, o:o + w], start=False, stop=final)
                        else:
                            mmv = t.matmul(ps_stat[:, o:o + w], lB[:, :], x2b[dc % 2][:, o:o + w],
                                           start=False, stop=final)
                    PE.inc(mmv)
                    pe_done[dc] = PE.n
                k.pe(p_pe)

            def f_dv(v):
                v.wait_ge(PE.h, pe_done[NCH - 1])
                if exact_scores:
                    k.dvs(v, v.tensor_scalar_mul(k.SCRO[0:1, :], ps_sc[0:1, :], 1.0))
                    k.dvs(v, v.tensor_scalar_mul(k.STROW[0:1, :], ps_sc[32:33, :], 1.0))
                    k.dvs(v, v.tensor_add(k.SCRO[0:1, :], k.SCRO[0:1, :], k.STROW[0:1, :]))
                k.dvs(v, v.tensor_scalar_mul(k.STM[0:1, :], ps_stat[0:1, :], 1.0 / D))
                k.dvs(v, v.tensor_scalar_mul(k.STROW[0:1, :], ps_stat[32:33, :], 1.0 / D))
                k.dvs(v, v.tensor_mul(k.STR[0:1, :], k.STM[0:1, :], k.STM[0:1, :]))
                k.dvs(v, v.tensor_sub(k.STROW[0:1, :], k.STROW[0:1, :], k.STR[0:1, :]))
                DV.inc(v.tensor_single_scalar(k.STROW2[0:1, :], k.STROW[0:1, :], float(eps), ALU.add))
            k.dv(f_dv)

            def f_ac(s):
                DV.wait(s)
                AC.inc(s.activation(k.STR[0:1, :], k.STROW2[0:1, :], AF.Sqrt))
            k.ac(f_ac)

            def f_dv2(v):
                AC.wait(v)
                k.dvs(v, v.reciprocal(k.STR[0:1, :], k.STR[0:1, :]))
                if exact_scores:
                    k.dvs(v, v.tensor_copy(k.STROW2[0:1, :], k.SCRO[0:1, :]))
                    k.dvs(v, v.tensor_single_scalar(k.STROW[0:1, :], k.STM[0:1, :], rsum1[0:1, 0:1], ALU.mult))
                    k.dvs(v, v.tensor_sub(k.SCRO[0:1, :], k.STROW2[0:1, :], k.STROW[0:1, :]))
                    k.dvs(v, v.tensor_copy(k.STROW[0:1, :], k.SCRO[0:1, :]))
                    k.dvs(v, v.tensor_mul(k.SCRO[0:1, :], k.STROW[0:1, :], k.STR[0:1, :]))
                DV.inc(v.tensor_copy(k.STBF[0:1, :], k.STM[0:1, :]))
            k.dv(f_dv2)
            if exact_scores:
                k._dbg_rsum1 = rsum1
                k.tap("rwh", rwh[:, :], [128, NCH], BF16)
                k.tap("rwl", rwl[:, :], [128, NCH], BF16)
                k.tap("rwf", rwf[:, :], [128, NCH], F32)
                k.tap("rsum", rsum[:, :], [128, 1], F32)
                k.tap("lcs", lCs[:, :], [128, NCH * 33], BF16)
                k.tap("lds", lDs[:, :], [128, NCH * 33], BF16)

        with nc.psum_tensor(f"ps_bc_{sq}", [128, TOK], F32) as ps_bc:
            def f_pe(t):
                DV.wait(t)
                mmv = None
                for (o, w) in mm_groups(TOK):
                    mmv = t.matmul(ps_bc[:, o:o + w], k.ONER[0:1, :], k.STBF[0:1, o:o + w],
                                   start=True, stop=True)
                PE.inc(mmv)
            k.pe(f_pe)

            def f_dv3(v):
                PE.wait(v)
                v.tensor_copy(k.MB[:, :], ps_bc[:, :])
                DV.inc(v.tensor_copy(k.STBF[0:1, :], k.STR[0:1, :]))
            k.dv(f_dv3)

            def f_pe2(t):
                DV.wait(t)
                mmv = None
                for (o, w) in mm_groups(TOK):
                    mmv = t.matmul(ps_bc[:, o:o + w], k.ONER[0:1, :], k.STBF[0:1, o:o + w],
                                   start=True, stop=True)
                PE.inc(mmv)
            k.pe(f_pe2)

            def f_dv4(v):
                PE.wait(v)
                v.tensor_copy(k.RB[:, :], ps_bc[:, :])
                last = None
                for dc in range(NCH):
                    sl = slice(dc * TOK, (dc + 1) * TOK)
                    v.tensor_sub(k.XLN[:, sl], k.XLN[:, sl], k.MB[:, :])
                    last = v.tensor_mul(k.XLN[:, sl], k.XLN[:, sl], k.RB[:, :])
                DV.inc(last)
            k.dv(f_dv4)




def precast_piece(k, io, name, lo=None, hi=None):
    nc = k.nc
    GP = k.sems["GP"]
    src_ap = io[name]
    if name + "_bf" not in io:
        io[name + "_bf"] = nc.dram_tensor(name + "_bf", list(src_ap.shape), BF16).ap()
    dst = io[name + "_bf"]
    sl = (slice(lo, hi),) + tuple(slice(0, s) for s in src_ap.shape[1:])

    def emit(g):
        GP.dma(g.dma_start(out=dst[sl], in_=src_ap[sl]))
    k.gp(emit)
    k.cast_done[name] = GP.n


def precast(k, io, names):
    nc = k.nc
    GP = k.sems["GP"]
    for name in names:
        src_ap = io[name]
        shp = list(src_ap.shape)
        dst = nc.dram_tensor(name + "_bf", shp, BF16).ap()
        io[name + "_bf"] = dst

        def emit(g, src_ap=src_ap, dst=dst):
            GP.dma(g.dma_start(out=dst[tuple(slice(0, s) for s in dst.shape)],
                               in_=src_ap[tuple(slice(0, s) for s in src_ap.shape)]))
        k.gp(emit)
        k.cast_done[name] = GP.n


def attn_a_ab(k, io):
    nc = k.nc
    LD, LD2, GP, CC, PE, DV, AC = (k.sems[n] for n in ("LD", "LD2", "GP", "CC", "PE", "DV", "AC"))
    k._wo_acc_hist = []
    aph = contextlib.ExitStack()
    k._attn_a_stack = aph

    qall_d = nc.dram_tensor("qall_d", [NH_A, QD_A, TOK], BF16).ap()
    k._qall_d = qall_d

    with contextlib.ExitStack() as ph:
        sbu = lambda n, s, d: ph.enter_context(nc.sbuf_tensor(n, list(s), d))
        WKV = sbu("WKV", [128, NCH * LORA], BF16)
        CKVO = sbu("CKVO", [128, TOK], BF16)

        def a_sync(e):
            k.fence(e)
            LD.dma(e.dma_start(out=WKV[:, :], in_=io["wkv_a_bf"][:, :]))
        k.sync(a_sync)
        ld_wkv = LD.n

        with nc.psum_tensor("ps_ckv", [128, TOK], F32) as ps_ckv:
            for Mc in range(4):
                def a_pe(t, Mc=Mc):
                    t.wait_ge(LD.h, ld_wkv)
                    DV.wait(t)
                    mmv = None
                    for dc in range(NCH):
                        lhs = WKV[:, dc * LORA + Mc * 128: dc * LORA + (Mc + 1) * 128]
                        for (o, w) in mm_groups(TOK):
                            mmv = t.matmul(ps_ckv[:, o:o + w], lhs,
                                           k.XLN[:, dc * TOK + o:dc * TOK + o + w],
                                           start=(dc == 0), stop=(dc == NCH - 1))
                    PE.inc(mmv)
                k.pe(a_pe)

                def a_dv(v, Mc=Mc):
                    PE.wait(v)
                    if Mc > 0:
                        LD2.wait(v)
                    DV.inc(v.tensor_copy(CKVO[:, :], ps_ckv[:, :]))
                k.dv(a_dv)

                def a_st(e, Mc=Mc):
                    DV.wait(e)
                    LD2.dma(e.dma_start(out=io["ckv_own_d"][Mc * 128:(Mc + 1) * 128, :],
                                        in_=CKVO[:, :]))
                k.sync(a_st)

        def a_cc(g):
            LD2.wait(g)
            CC.inc(g.collective_compute(
                "AllGather", ALU.bypass,
                replica_groups=[[0, 1], [2, 3], [4, 5], [6, 7]],
                ins=[io["ckv_own_d"][:, :]], outs=[io["ckv_all_d"][:, :]]), 1)
        k.gp(a_cc)

    CKV = aph.enter_context(nc.sbuf_tensor("CKV", [128, 4 * S], BF16, side="right"))
    k._ckv_sb = CKV

    def ckv_ld(e):
        CC.wait(e)
        for cc in range(4):
            for r in range(2):
                LD.dma(e.dma_start(
                    out=CKV[:, cc * S + r * TOK: cc * S + (r + 1) * TOK],
                    in_=io["ckv_all_d"][r * 512 + cc * 128: r * 512 + (cc + 1) * 128, :]))
    k.sync(ckv_ld)
    ld_ckv = LD.n
    k._ld_ckv = ld_ckv

    MXQ = aph.enter_context(nc.sbuf_tensor("MXQ", [1, NH_A], F32, side="right"))
    k._mxq_sb = MXQ
    with contextlib.ExitStack() as ph:
        sbu = lambda n, s, d: ph.enter_context(nc.sbuf_tensor(n, list(s), d))
        WQA = sbu("WQA", [128, NCH * QD_A], BF16)
        QSTG = sbu("QSTG", [128, TOK], BF16)
        Q2R = sbu("Q2R", [64, TOK], BF16)
        Q2S = sbu("Q2S", [64, TOK], BF16)
        QSQ = sbu("QSQ", [128, TOK], BF16)
        QSCR = sbu("QSCR", [64, TOK], BF16)
        RQCS = sbu("RQCS", [64, TOK], BF16)
        RQSC = sbu("RQSC", [64, TOK], BF16)
        NRM = sbu("NRM", [1, 4], F32)

        def rope_ld(e):
            k.fence(e)
            LD.dma(e.dma_start(out=RQCS[:, :], in_=io["ropeq_cs"][:, :]))
            LD.dma(e.dma_start(out=RQSC[:, :], in_=io["ropeq_sc"][:, :]))
        k.sync(rope_ld)

        with nc.psum_tensor("ps_q", [128, TOK], F32) as ps_q, \
             nc.psum_tensor("ps_nr", [1, 512], F32) as ps_nr:
            prev_dv = {}
            prev_st = {}
            for h in range(NH_A):
                def q_ld(e, h=h):
                    e.wait_ge(GP.h, k.cast_done["wq_a"])
                    if h > 0:
                        e.wait_ge(PE.h, prev_dv[h - 1][0])
                    LD.dma(e.dma_start(out=WQA[:, :], in_=io["wq_a_bf"][h, :, :]))
                k.sync(q_ld)
                ld_q = LD.n

                def q_pe(t, h=h):
                    t.wait_ge(LD.h, ld_q)
                    if h > 0:
                        t.wait_ge(DV.h, prev_dv[h - 1][1])
                    mmv = None
                    for dc in range(NCH):
                        lhs = WQA[:, dc * QD_A: dc * QD_A + 128]
                        for (o, w) in mm_groups(TOK):
                            mmv = t.matmul(ps_q[:, o:o + w], lhs,
                                           k.XLN[:, dc * TOK + o:dc * TOK + o + w],
                                           start=(dc == 0), stop=(dc == NCH - 1))
                    PE.inc(mmv)
                k.pe(q_pe)
                pe_nope = PE.n

                def q_dv(v, h=h):
                    v.wait_ge(PE.h, pe_nope)
                    if h > 0:
                        LD2.wait(v)
                        AC.wait(v)
                    DV.inc(v.tensor_copy(QSTG[:, :], ps_q[:, :]))
                k.dv(q_dv)
                dv_nope = DV.n

                def q_pe2(t, h=h):
                    t.wait_ge(DV.h, dv_nope)
                    mmv = None
                    for dc in range(NCH):
                        lhs = WQA[:, dc * QD_A + 128: dc * QD_A + QD_A]
                        for (o, w) in mm_groups(TOK):
                            mmv = t.matmul(ps_q[0:64, o:o + w], lhs,
                                           k.XLN[:, dc * TOK + o:dc * TOK + o + w],
                                           start=(dc == 0), stop=(dc == NCH - 1))
                    PE.inc(mmv)
                k.pe(q_pe2)
                pe_rope = PE.n

                def q_dv2(v, h=h):
                    v.wait_ge(PE.h, pe_rope)
                    AC.wait(v)
                    v.tensor_copy(Q2R[:, :], ps_q[0:64, :])
                    v.tensor_mul(QSCR[0:32, :], Q2R[32:64, :], RQCS[32:64, :])
                    v.tensor_mul(Q2S[0:32, :], Q2R[0:32, :], RQCS[0:32, :])
                    v.tensor_sub(Q2S[0:32, :], Q2S[0:32, :], QSCR[0:32, :])
                    v.tensor_mul(QSCR[32:64, :], Q2R[0:32, :], RQSC[0:32, :])
                    v.tensor_mul(Q2S[32:64, :], Q2R[32:64, :], RQSC[32:64, :])
                    DV.inc(v.tensor_add(Q2S[32:64, :], Q2S[32:64, :], QSCR[32:64, :]))
                k.dv(q_dv2)
                dv_rope = DV.n
                prev_dv[h] = (pe_rope, dv_rope)
                prev_st[h] = LD2.n

                def q_st(e, h=h):
                    e.wait_ge(DV.h, dv_rope)
                    LD2.dma(e.dma_start(out=qall_d[h, 0:128, :], in_=QSTG[:, :]))
                    LD2.dma(e.dma_start(out=qall_d[h, 128:QD_A, :], in_=Q2S[:, :]))
                k.sync(q_st)
                st_q = LD2.n

                def q_ac(s, h=h):
                    s.wait_ge(PE.h, pe_rope)
                    s.wait_ge(DV.h, dv_rope)
                    s.activation(QSQ[:, :], QSTG[:, :], AF.Square)
                    AC.inc(s.activation(QSCR[0:64, :], Q2R[:, :], AF.Square))
                k.ac(q_ac)
                ac_sq = AC.n

                for j, (o, w) in enumerate(mm_groups(TOK)):
                    def n_pe(t, o=o, w=w, j=j, h=h):
                        t.wait_ge(AC.h, ac_sq)
                        if j > 0 or h > 0:
                            DV.wait(t)
                        t.matmul(ps_nr[0:1, 0:w], k.ONEC[:, :], QSQ[:, o:o + w],
                                 start=True, stop=False)
                        PE.inc(t.matmul(ps_nr[0:1, 0:w], k.ONEC[0:64, :], QSCR[0:64, o:o + w],
                                        start=False, stop=True))
                    k.pe(n_pe)

                    def n_dv(v, j=j, h=h, w=w):
                        PE.wait(v)
                        k.dvs(v, v.reduce_max(NRM[0:1, 0:1], ps_nr[0:1, 0:w], axis=AX.X))
                        if j == 0:
                            DV.inc(v.tensor_copy(MXQ[0:1, h:h + 1], NRM[0:1, 0:1]))
                        else:
                            k.dvs(v, v.tensor_tensor(NRM[0:1, 1:2], MXQ[0:1, h:h + 1],
                                                     NRM[0:1, 0:1], ALU.max))
                            DV.inc(v.tensor_copy(MXQ[0:1, h:h + 1], NRM[0:1, 1:2]))
                    k.dv(n_dv)


def attn_a_c(k, io):
    nc = k.nc
    LD, LD2, GP, CC, PE, DV, AC = (k.sems[n] for n in ("LD", "LD2", "GP", "CC", "PE", "DV", "AC"))
    aph = k._attn_a_stack
    qall_d = k._qall_d
    CKV = k._ckv_sb
    MXQ = k._mxq_sb
    ld_ckv = k._ld_ckv
    SC = 1.0 / float(np.sqrt(QD_A))
    with contextlib.ExitStack() as ph:
        sbu = lambda n, s, d: ph.enter_context(nc.sbuf_tensor(n, list(s), d))
        KH1 = sbu("KH1", [128, S], BF16)
        KH2R = sbu("KH2R", [64, S], BF16)
        KH2 = sbu("KH2", [64, S], BF16)
        VT = sbu("VT", [128, S], BF16)
        SQS = sbu("SQS", [128, TOK], BF16)
        SQS2 = sbu("SQS2", [64, TOK], BF16)
        QH = sbu("QH", [128, TOK], BF16)
        QH2 = sbu("QH2", [64, TOK], BF16)
        EXPP = sbu("EXPP", [128, 2048], BF16)
        DEN = sbu("DEN", [128, 1024], F32)
        OAVR = sbu("OAVR", [128, 1024], BF16)
        OAV = sbu("OAV", [128, 4 * TOK], BF16)
        WKU = sbu("WKU", [128, 4 * QD_A], BF16)
        WVU = sbu("WVU", [128, 4 * VD_A], BF16)
        WOG = sbu("WOG", [128, 4 * D], BF16)
        RKCS = sbu("RKCS", [64, S], BF16)
        RKSC = sbu("RKSC", [64, S], BF16)
        DSTA = sbu("DSTA", [128, TOK], F32)
        RDH = sbu("RDH", [1, 1024], BF16)
        RDL = sbu("RDL", [1, 1024], BF16)
        MXK = sbu("MXK", [1, 4], F32)
        NEGC = sbu("NEGC", [1, 4], F32)
        NEGCB = sbu("NEGCB", [128, 1], F32)
        NEGCBF = sbu("NEGCBF", [1, 4], BF16)
        DST = [DSTA[:, 0:1024], DSTA[:, 1024:2048]]

        ps_big = ph.enter_context(nc.psum_tensor("ps_big", [128, 2048], F32))
        ps_av = ph.enter_context(nc.psum_tensor("ps_av", [128, 1024], F32))
        ps_nr2 = ph.enter_context(nc.psum_tensor("ps_nr2", [128, 512], F32))
        ps_t = ph.enter_context(nc.psum_tensor("ps_t", [128, 256], BF16))

        def rk_ld(e):
            k.fence(e)
            LD.dma(e.dma_start(out=RKCS[:, :], in_=io["ropek_cs"][:, :]))
            LD.dma(e.dma_start(out=RKSC[:, :], in_=io["ropek_sc"][:, :]))
            e.wait_ge(GP.h, max(k.cast_done["wk_up_a"], k.cast_done["wv_up_a"]))
        k.sync(rk_ld)

        accum_done = {}
        pe_head_end = 0
        dv_head_end = 0
        castB = ([("wq_f", None, None), ("wk_f", None, None), ("wv_f", None, None),
                  ("wo_f", None, None)] +
                 [("wg_f", 16 * i, 16 * (i + 1)) for i in range(4)] +
                 [("wu_f", 16 * i, 16 * (i + 1)) for i in range(4)] +
                 [("wd_f", 16 * i, 16 * (i + 1)) for i in range(4)])
        for h in range(NH_A):
            if h < len(castB):
                precast_piece(k, io, *castB[h])
            g = h // 4
            gslot = h % 4

            def h_ld(e, h=h):
                e.wait_ge(PE.h, pe_head_end)
                LD2.wait(e)
                LD.dma(e.dma_start(out=WKU[:, :], in_=io["wk_up_a_bf"][h, :, :]))
                LD.dma(e.dma_start(out=WVU[:, :], in_=io["wv_up_a_bf"][h, :, :]))
                LD.dma(e.dma_start(out=QH[:, :], in_=qall_d[h, 0:128, :]))
                LD.dma(e.dma_start(out=QH2[:, :], in_=qall_d[h, 128:QD_A, :]))
            k.sync(h_ld)
            ld_h = LD.n

            for kh in range(2):
                ksl = slice(kh * TOK, (kh + 1) * TOK)

                def kj_pe(t, h=h, kh=kh):
                    t.wait_ge(LD.h, ld_h)
                    t.wait_ge(LD.h, ld_ckv)
                    DV.wait(t)
                    mmv = None
                    for cc in range(4):
                        lhs = WKU[:, cc * QD_A: cc * QD_A + 128]
                        for (o, w) in mm_groups(TOK):
                            mmv = t.matmul(ps_big[:, o:o + w], lhs,
                                           CKV[:, cc * S + kh * TOK + o: cc * S + kh * TOK + o + w],
                                           start=(cc == 0), stop=(cc == 3))
                    PE.inc(mmv)
                k.pe(kj_pe)
                pe_k1 = PE.n

                def kj_dv(v, h=h, kh=kh):
                    v.wait_ge(PE.h, pe_k1)
                    DV.inc(v.tensor_copy(KH1[:, ksl], ps_big[:, :]))
                k.dv(kj_dv)
                dv_k1 = DV.n

                def kj_ac(s, h=h, kh=kh):
                    s.wait_ge(PE.h, pe_k1)
                    AC.inc(s.activation(SQS[:, :], ps_big[:, :], AF.Square))
                k.ac(kj_ac)
                ac_k1 = AC.n

                def k2_pe(t, h=h, kh=kh):
                    t.wait_ge(DV.h, dv_k1)
                    t.wait_ge(AC.h, ac_k1)
                    mmv = None
                    for cc in range(4):
                        lhs = WKU[:, cc * QD_A + 128: (cc + 1) * QD_A]
                        for (o, w) in mm_groups(TOK):
                            mmv = t.matmul(ps_big[0:64, o:o + w], lhs,
                                           CKV[:, cc * S + kh * TOK + o: cc * S + kh * TOK + o + w],
                                           start=(cc == 0), stop=(cc == 3))
                    PE.inc(mmv)
                k.pe(k2_pe)
                pe_k2 = PE.n

                def k2_dv(v, h=h, kh=kh):
                    v.wait_ge(PE.h, pe_k2)
                    DV.inc(v.tensor_copy(KH2R[:, ksl], ps_big[0:64, :]))
                k.dv(k2_dv)
                dv_k2 = DV.n

                def k2_ac(s, h=h, kh=kh):
                    s.wait_ge(PE.h, pe_k2)
                    s.wait_ge(DV.h, dv_k2)
                    AC.inc(s.activation(SQS2[:, :], ps_big[0:64, :], AF.Square))
                k.ac(k2_ac)
                ac_k2 = AC.n

                for j, (o, w) in enumerate(mm_groups(TOK)):
                    def nk_pe(t, o=o, w=w, j=j, kh=kh, h=h):
                        t.wait_ge(AC.h, ac_k2)
                        DV.wait(t)
                        t.matmul(ps_nr2[0:1, 0:w], k.ONEC[:, :], SQS[:, o:o + w],
                                 start=True, stop=False)
                        PE.inc(t.matmul(ps_nr2[0:1, 0:w], k.ONEC[0:64, :], SQS2[0:64, o:o + w],
                                        start=False, stop=True))
                    k.pe(nk_pe)

                    def nk_dv(v, j=j, kh=kh, h=h, w=w):
                        PE.wait(v)
                        k.dvs(v, v.reduce_max(MXK[0:1, 1:2], ps_nr2[0:1, 0:w], axis=AX.X))
                        if j == 0 and kh == 0:
                            DV.inc(v.tensor_copy(MXK[0:1, 0:1], MXK[0:1, 1:2]))
                        else:
                            DV.inc(v.tensor_tensor(MXK[0:1, 2:3], MXK[0:1, 0:1],
                                                   MXK[0:1, 1:2], ALU.max))
                            v.wait_ge(DV.h, DV.n)
                            DV.inc(v.tensor_copy(MXK[0:1, 0:1], MXK[0:1, 2:3]))
                    k.dv(nk_dv)

                def kr_dv(v, h=h, kh=kh):
                    PE.wait(v)
                    v.tensor_mul(SQS2[0:32, :], KH2R[32:64, ksl], RKCS[32:64, ksl])
                    v.tensor_mul(KH2[0:32, ksl], KH2R[0:32, ksl], RKCS[0:32, ksl])
                    v.tensor_sub(KH2[0:32, ksl], KH2[0:32, ksl], SQS2[0:32, :])
                    v.tensor_mul(SQS2[32:64, :], KH2R[0:32, ksl], RKSC[0:32, ksl])
                    v.tensor_mul(KH2[32:64, ksl], KH2R[32:64, ksl], RKSC[32:64, ksl])
                    DV.inc(v.tensor_add(KH2[32:64, ksl], KH2[32:64, ksl], SQS2[32:64, :]))
                k.dv(kr_dv)

                def vj_pe(t, h=h, kh=kh):
                    DV.wait(t)
                    mmv = None
                    for cc in range(4):
                        lhs = WVU[:, cc * VD_A: (cc + 1) * VD_A]
                        for (o, w) in mm_groups(TOK):
                            mmv = t.matmul(ps_big[:, o:o + w], lhs,
                                           CKV[:, cc * S + kh * TOK + o: cc * S + kh * TOK + o + w],
                                           start=(cc == 0), stop=(cc == 3))
                    PE.inc(mmv)
                k.pe(vj_pe)
                pe_v = PE.n

                def vj_dv(v, h=h, kh=kh):
                    v.wait_ge(PE.h, pe_v)
                    AC.wait(v)
                    DV.inc(v.tensor_copy(SQS[:, :], ps_big[:, :]))
                k.dv(vj_dv)
                dv_v = DV.n

                for tt in range(16):
                    def vt_pe(t, tt=tt, kh=kh):
                        t.wait_ge(DV.h, dv_v if tt < 2 else DV.n)
                        PE.inc(t.transpose(ps_t[:, (tt % 2) * 128:(tt % 2) * 128 + 128],
                                           SQS[:, tt * 128:(tt + 1) * 128], k.IDENT[:, :]))
                    k.pe(vt_pe)

                    def vt_dv(v, tt=tt, kh=kh):
                        PE.wait(v)
                        DV.inc(v.tensor_copy(VT[:, (kh * 16 + tt) * 128:(kh * 16 + tt) * 128 + 128],
                                             ps_t[:, (tt % 2) * 128:(tt % 2) * 128 + 128]))
                    k.dv(vt_dv)

            def c_dv(v, h=h):
                DV.wait(v)
                DV.inc(v.tensor_mul(NEGC[0:1, 1:2], MXK[0:1, 0:1], MXQ[0:1, h:h + 1]))
            k.dv(c_dv)

            def c_ac(s, h=h):
                DV.wait(s)
                AC.inc(s.activation(NEGC[0:1, 2:3], NEGC[0:1, 1:2], AF.Sqrt, scale=1.0 / QD_A))
            k.ac(c_ac)

            def c_dv2(v, h=h):
                AC.wait(v)
                k.dvs(v, v.tensor_scalar(NEGC[0:1, 0:1], NEGC[0:1, 2:3], -1.0, -1.0, ALU.mult, ALU.add))
                DV.inc(v.tensor_copy(NEGCBF[0:1, 0:1], NEGC[0:1, 0:1]))
            k.dv(c_dv2)

            def c_pe(t, h=h):
                DV.wait(t)
                PE.inc(t.matmul(ps_nr2[:, 0:1], k.ONER[0:1, :], NEGCBF[0:1, 0:1],
                                start=True, stop=True))
            k.pe(c_pe)

            def c_dv3(v, h=h):
                PE.wait(v)
                AC.wait(v)
                DV.inc(v.tensor_copy(NEGCB[:, :], ps_nr2[:, 0:1]))
            k.dv(c_dv3)
            dv_negc = DV.n

            for qh in range(2):
                qsl = slice(qh * 1024, (qh + 1) * 1024)

                def att_init(v, h=h, qh=qh):
                    v.wait_ge(DV.h, dv_negc)
                    PE.wait(v)
                    DV.inc(v.memset(DEN[:, :], 0.0))
                k.dv(att_init)

                ac_exp = {}
                pe_sc = {}
                dv_den = {}
                pe_av = {}
                for kc in range(32):
                    par = kc % 2
                    pssl = slice(par * 1024, (par + 1) * 1024)

                    def sc_pe(t, h=h, qh=qh, kc=kc, par=par, pssl=pssl):
                        if kc < 2:
                            DV.wait(t)
                        else:
                            t.wait_ge(AC.h, ac_exp[kc - 2])
                        mmv = None
                        for (o, w) in mm_groups(1024):
                            for li, (lhs1, rhs1) in enumerate((
                                (KH1[:, kc * 128:(kc + 1) * 128], QH),
                                (KH2[0:64, kc * 128:(kc + 1) * 128], QH2),
                            )):
                                mmv = t.matmul(ps_big[:, pssl.start + o: pssl.start + o + w],
                                               lhs1, rhs1[:, qh * 1024 + o: qh * 1024 + o + w],
                                               start=(li == 0), stop=(li == 1))
                        PE.inc(mmv)
                        pe_sc[kc] = PE.n
                    k.pe(sc_pe)

                    def ex_ac(s, h=h, qh=qh, kc=kc, par=par, pssl=pssl):
                        s.wait_ge(PE.h, pe_sc[kc])
                        if kc >= 2:
                            s.wait_ge(DV.h, dv_den[kc - 2])
                            s.wait_ge(PE.h, pe_av[kc - 2])
                        AC.inc(s.activation(EXPP[:, pssl], ps_big[:, pssl], AF.Exp,
                                            bias=NEGCB[:, 0:1], scale=SC))
                        ac_exp[kc] = AC.n
                    k.ac(ex_ac)

                    def den_dv(v, h=h, qh=qh, kc=kc, pssl=pssl):
                        v.wait_ge(AC.h, ac_exp[kc])
                        DV.inc(v.tensor_add(DEN[:, :], DEN[:, :], EXPP[:, pssl]))
                        dv_den[kc] = DV.n
                    k.dv(den_dv)

                    def av_pe(t, h=h, qh=qh, kc=kc, pssl=pssl):
                        t.wait_ge(AC.h, ac_exp[kc])
                        mmv = None
                        for (o, w) in mm_groups(1024):
                            mmv = t.matmul(ps_av[:, o:o + w], VT[:, kc * 128:(kc + 1) * 128],
                                           EXPP[:, pssl.start + o: pssl.start + o + w],
                                           start=(kc == 0), stop=(kc == 31))
                        PE.inc(mmv)
                        pe_av[kc] = PE.n
                    k.pe(av_pe)

                for sl in range(2):
                    def dn_pe(t, h=h, qh=qh, sl=sl):
                        t.wait_ge(DV.h, dv_den[31])
                        if sl == 1:
                            DV.wait(t)
                        PE.inc(t.matmul(ps_nr2[0:1, 0:512], k.ONEC32[:, :],
                                        DEN[:, sl * 512:(sl + 1) * 512], start=True, stop=True))
                    k.pe(dn_pe)

                    def dn_dv(v, h=h, qh=qh, sl=sl):
                        PE.wait(v)
                        DV.inc(v.reciprocal(DEN[0:1, sl * 512:(sl + 1) * 512],
                                            ps_nr2[0:1, 0:512]))
                    k.dv(dn_dv)

                def rd_dv(v, h=h, qh=qh):
                    DV.wait(v)
                    k.dvs(v, v.tensor_copy(RDH[0:1, :], DEN[0:1, 0:1024]))
                    k.dvs(v, v.tensor_sub(DEN[0:1, 0:1024], DEN[0:1, 0:1024], RDH[0:1, :]))
                    k.dvs(v, v.tensor_copy(RDL[0:1, :], DEN[0:1, 0:1024]))
                    v.wait_ge(PE.h, pe_av[31])
                    DV.inc(v.tensor_copy(OAVR[:, :], ps_av[:, :]))
                k.dv(rd_dv)

                def bc_pe(t, h=h, qh=qh):
                    DV.wait(t)
                    t.wait_ge(AC.h, ac_exp[31])
                    mmv = None
                    for (o, w) in mm_groups(1024):
                        t.matmul(ps_big[:, o:o + w], k.ONER[0:1, :], RDH[0:1, o:o + w],
                                 start=True, stop=False)
                        mmv = t.matmul(ps_big[:, o:o + w], k.ONER[0:1, :], RDL[0:1, o:o + w],
                                       start=False, stop=True)
                    PE.inc(mmv)
                k.pe(bc_pe)

                def nm_dv(v, h=h, qh=qh, gslot=gslot):
                    PE.wait(v)
                    DV.inc(v.tensor_mul(OAV[:, gslot * TOK + qh * 1024: gslot * TOK + (qh + 1) * 1024],
                                        OAVR[:, :], ps_big[:, 0:1024]))
                k.dv(nm_dv)

            pe_head_end = PE.n
            dv_head_end = DV.n

            if gslot == 3:
                def wo_ld(e, g=g):
                    e.wait_ge(GP.h, k.cast_done["wo_a"])
                    e.wait_ge(PE.h, pe_head_end)
                    LD.dma(e.dma_start(out=WOG[:, :], in_=io["wo_a_bf"][g, :, :]))
                k.sync(wo_ld)
                ld_wo = LD.n

                for dc in range(NCH):
                    def wo_pe(t, g=g, dc=dc):
                        t.wait_ge(LD.h, ld_wo)
                        DV.wait(t)
                        mmv = None
                        for (o, w) in mm_groups(TOK):
                            for i in range(4):
                                mmv = t.matmul(ps_big[:, o:o + w],
                                               WOG[:, i * D + dc * 128:(i * D + dc * 128) + 128],
                                               OAV[:, i * TOK + o:i * TOK + o + w],
                                               start=(i == 0), stop=(i == 3))
                        PE.inc(mmv)
                    k.pe(wo_pe)
                    pe_wo = PE.n

                    def wo_dv(v, g=g, dc=dc):
                        v.wait_ge(PE.h, pe_wo)
                        if len(k._wo_acc_hist) >= 1:
                            v.wait_ge(GP.h, k._wo_acc_hist[-1])
                        DV.inc(v.tensor_copy(DSTA[:, :], ps_big[:, :]))
                    k.dv(wo_dv)
                    dv_wo = DV.n

                    def wo_gp(gp, g=g, dc=dc):
                        gp.wait_ge(DV.h, dv_wo)
                        if g > 0:
                            gp.wait_ge(GP.h, accum_done[dc])
                        GP.dma(gp.dma_start(out=io["xout"][dc, :, :], in_=DSTA[:, :],
                                            accum_op=ALU.add))
                        accum_done[dc] = GP.n
                        k._wo_acc_hist.append(GP.n)
                    k.gp(wo_gp)
    k.attn_a_accum_done = dict(accum_done)
    aph.close()


class RouterBufs:
    pass


def ffn_router_alloc(k, surv_stack, temp_stack, sfx=""):
    nc = k.nc
    r = RouterBufs()
    sbS = lambda n, s, d: surv_stack.enter_context(nc.sbuf_tensor(n + sfx, list(s), d, side="right"))
    sb = lambda n, s, d: temp_stack.enter_context(nc.sbuf_tensor(n + sfx, list(s), d))
    r.SCT = sb("r_SCT", [128, 32], F32)
    r.POS = sb("r_POS", [128, 32], F32)
    r.GE = sb("r_GE", [128, 32], F32)
    r.EQ = sb("r_EQ", [128, 32], F32)
    r.AND = sb("r_AND", [128, 32], F32)
    r.SEL = sb("r_SEL", [128, 32], F32)
    r.PRE = sb("r_PRE", [128, 32], F32)
    r.PR2 = sb("r_PR2", [128, 32], F32)
    r.RANK = sb("r_RANK", [128, 32], F32)
    r.RC = sb("r_RC", [128, 16], F32)
    r.RCB = sb("r_RCB", [128, 2], BF16)
    r.JMI = sb("r_JMI", [128, 1], I32)
    r.ROW = sb("r_ROW", [1, 16], F32)
    r.ROWB = sb("r_ROWB", [1, 16], BF16)
    r.SC2 = sb("r_SC2", [128, 64], BF16)
    r.ONES2 = sb("r_ONES2", [128, 128], BF16)
    r.TRIW = sb("r_TRIW", [128, 128], BF16)
    r.JF = sb("r_JF", [128, 512], F32)
    r.JOWN = sb("r_JOWN", [128, 256], F32)
    r.VALS = sb("r_VALS", [128, 96], F32)
    r.STIDX = sb("r_STIDX", [128, 32], F32)
    r.PT6 = sb("r_PT6", [128, 768], F32)
    r.IDXS = sbS("r_IDXS", [128, 12], F32)
    r.OWNI = sbS("r_OWNI", [128, 6], F32)
    r.IDXI = sbS("r_IDXI", [128, 8], I32)
    return r


def ffn_router(k, io, call):
    nc = k.nc
    LD, LD2, GP, CC, PE, DV, AC = (k.sems[n] for n in ("LD", "LD2", "GP", "CC", "PE", "DV", "AC"))
    r = k._r
    sc_own_d, sc_all_d = io["sc_own_d"], io["sc_all_d"]

    def r_st(e):
        k.fence(e)
        LD2.dma(e.dma_start(out=sc_own_d[0:1, :], in_=k.SCRO[0:1, :]))
        LD.dma(e.dma_start(out=r.STIDX[:, :], in_=io["st_idx"][:, :]))
        LD.dma(e.dma_start(out=r.TRIW[:, :], in_=io["triw"][:, :]))
        LD.dma(e.dma_start(out=r.JOWN[:, :], in_=io["jown"][:, :]))
    k.sync(r_st)

    def r_cc(g):
        LD2.wait(g)
        CC.inc(g.collective_compute(
            "AllGather", ALU.bypass,
            replica_groups=[[0, 1], [2, 3], [4, 5], [6, 7]],
            ins=[sc_own_d[:, :]], outs=[sc_all_d[:, :]]), 1)
    k.gp(r_cc)

    def r_ld(e):
        k.fence(e)
        flat = sc_all_d.rearrange("a b -> () (a b)")
        for (a, l, sa) in S2T_RANGES:
            p0, np_ = a // 32, l // 32
            off = 0
            while off < np_:
                take = min(32, np_ - off)
                pp = p0 + off
                LD.dma(e.dma_start(
                    out=r.SCT[pp:pp + take, 0:32],
                    in_=flat[0:1, sa + off * 32: sa + (off + take) * 32]
                        .rearrange("o (p t) -> (o p) t", p=take)))
                off += take
    k.sync(r_ld)
    ld_sct = LD.n

    def r_gp(g):
        k.fence(g)
        g.iota(r.POS[:, :], pattern=[[1, 32]], base=0, channel_multiplier=32,
               allow_small_or_imprecise_dtypes=True)
        GP.inc(g.iota(r.JF[:, :], pattern=[[1, 512]], base=1, channel_multiplier=0,
                      allow_small_or_imprecise_dtypes=True), 1)
    k.gp(r_gp)

    with nc.psum_tensor(f"ps_r{call}", [128, 4], F32) as ps_r, \
         nc.psum_tensor(f"ps_rt{call}", [64, 128], BF16) as ps_rt, \
         nc.psum_tensor(f"ps_rx{call}", [128, 18], F32) as ps_rx:
        def r_init(v):
            v.wait_ge(LD.h, ld_sct)
            GP.wait(v)
            LD2.wait(v)
            v.memset(r.ONES2[:, :], 1.0)
            v.memset(r.SC2[:, :], 0.0)
            k.dvs(v, v.reduce_max(r.RC[:, 12:13], r.SCT[:, :], axis=AX.X))
            k.dvs(v, v.tensor_reduce(r.RC[:, 13:14], r.SCT[:, :], axis=AX.X, op=ALU.min))
            k.dvs(v, v.tensor_copy(r.SC2[:, 0:1], r.RC[:, 12:13]))
            DV.inc(v.tensor_copy(r.SC2[:, 32:33], r.RC[:, 13:14]))
        k.dv(r_init)

        def r_pe0(t):
            DV.wait(t)
            PE.inc(t.transpose(ps_rt[:, :], r.SC2[:, :], k.IDENT[:, :]))
        k.pe(r_pe0)
        pe_t0 = PE.n

        def r_dv0(v):
            v.wait_ge(PE.h, pe_t0)
            k.dvs(v, v.reduce_max(r.ROW[0:1, 0:1], ps_rt[0:1, 0:128], axis=AX.X))
            k.dvs(v, v.tensor_reduce(r.ROW[0:1, 1:2], ps_rt[32:33, 0:128], axis=AX.X, op=ALU.min))
            k.dvs(v, v.tensor_single_scalar(r.ROW[0:1, 2:3], r.ROW[0:1, 0:1], 1.0, ALU.add))
            k.dvs(v, v.tensor_single_scalar(r.ROW[0:1, 3:4], r.ROW[0:1, 1:2], -1.0, ALU.add))
            k.dvs(v, v.tensor_copy(r.ROWB[0:1, 2:3], r.ROW[0:1, 2:3]))
            DV.inc(v.tensor_copy(r.ROWB[0:1, 3:4], r.ROW[0:1, 3:4]))
        k.dv(r_dv0)

        def r_pe1(t):
            DV.wait(t)
            t.matmul(ps_r[:, 0:1], k.ONER[0:1, :], r.ROWB[0:1, 3:4], start=True, stop=True)
            PE.inc(t.matmul(ps_r[:, 1:2], k.ONER[0:1, :], r.ROWB[0:1, 2:3], start=True, stop=True))
        k.pe(r_pe1)
        pe_bc = PE.n

        def r_dv1(v):
            v.wait_ge(PE.h, pe_bc)
            v.tensor_copy(r.RC[:, 0:1], ps_r[:, 0:1])
            DV.inc(v.tensor_copy(r.RC[:, 1:2], ps_r[:, 1:2]))
        k.dv(r_dv1)

        N_B1 = 45
        pe_cnt = [0]
        for it in range(N_B1):
            def b_dv(v, it=it):
                if it > 0:
                    v.wait_ge(PE.h, pe_cnt[0])
                    k.dvs(v, v.tensor_single_scalar(r.RC[:, 3:4], ps_r[:, 2:3], 512.0, ALU.is_ge))
                    k.dvs(v, v.tensor_sub(r.RC[:, 12:13], r.RC[:, 2:3], r.RC[:, 0:1]))
                    k.dvs(v, v.tensor_mul(r.RC[:, 13:14], r.RC[:, 12:13], r.RC[:, 3:4]))
                    k.dvs(v, v.tensor_add(r.RC[:, 14:15], r.RC[:, 0:1], r.RC[:, 13:14]))
                    k.dvs(v, v.tensor_sub(r.RC[:, 12:13], r.RC[:, 1:2], r.RC[:, 2:3]))
                    k.dvs(v, v.tensor_mul(r.RC[:, 13:14], r.RC[:, 12:13], r.RC[:, 3:4]))
                    k.dvs(v, v.tensor_add(r.RC[:, 1:2], r.RC[:, 2:3], r.RC[:, 13:14]))
                    k.dvs(v, v.tensor_copy(r.RC[:, 0:1], r.RC[:, 14:15]))
                k.dvs(v, v.tensor_add(r.RC[:, 12:13], r.RC[:, 0:1], r.RC[:, 1:2]))
                k.dvs(v, v.tensor_scalar_mul(r.RC[:, 2:3], r.RC[:, 12:13], 0.5))
                k.dvs(v, v.tensor_scalar(r.GE[:, :], r.SCT[:, :], r.RC[:, 2:3], 0.0,
                                         ALU.is_ge, ALU.add, accum_out=r.RC[:, 4:5]))
                DV.inc(v.tensor_copy(r.RCB[:, 0:1], r.RC[:, 4:5]))
            k.dv(b_dv)
            dv_b = DV.n

            def b_pe(t, it=it):
                t.wait_ge(DV.h, dv_b)
                PE.inc(t.matmul(ps_r[:, 2:3], r.ONES2[:, :], r.RCB[:, 0:1],
                                start=True, stop=True))
                pe_cnt[0] = PE.n
            k.pe(b_pe)

        def b_fin(v):
            v.wait_ge(PE.h, pe_cnt[0])
            k.dvs(v, v.tensor_single_scalar(r.RC[:, 3:4], ps_r[:, 2:3], 512.0, ALU.is_ge))
            k.dvs(v, v.tensor_sub(r.RC[:, 12:13], r.RC[:, 2:3], r.RC[:, 0:1]))
            k.dvs(v, v.tensor_mul(r.RC[:, 13:14], r.RC[:, 12:13], r.RC[:, 3:4]))
            k.dvs(v, v.tensor_add(r.RC[:, 5:6], r.RC[:, 0:1], r.RC[:, 13:14]))
            k.dvs(v, v.tensor_scalar(r.GE[:, :], r.SCT[:, :], r.RC[:, 5:6], 0.0,
                                     ALU.is_gt, ALU.add, accum_out=r.RC[:, 4:5]))
            k.dvs(v, v.tensor_single_scalar(r.EQ[:, :], r.SCT[:, :], r.RC[:, 5:6], ALU.is_equal))
            DV.inc(v.tensor_copy(r.RCB[:, 0:1], r.RC[:, 4:5]))
        k.dv(b_fin)
        dv_f = DV.n

        def f_pe(t):
            t.wait_ge(DV.h, dv_f)
            PE.inc(t.matmul(ps_r[:, 2:3], r.ONES2[:, :], r.RCB[:, 0:1], start=True, stop=True))
        k.pe(f_pe)
        pe_mgt = PE.n

        def f_dv(v):
            v.wait_ge(PE.h, pe_mgt)
            k.dvs(v, v.tensor_scalar(r.RC[:, 6:7], ps_r[:, 2:3], -1.0, 512.0, ALU.mult, ALU.add))
            v.memset(r.RC[:, 7:8], -1.0)
            DV.inc(v.memset(r.RC[:, 8:9], 4095.0))
        k.dv(f_dv)

        N_B2 = 14
        pe_cnt2 = [0]
        for it in range(N_B2):
            def j_dv(v, it=it):
                if it > 0:
                    v.wait_ge(PE.h, pe_cnt2[0])
                    k.dvs(v, v.tensor_tensor(r.RC[:, 3:4], ps_r[:, 3:4], r.RC[:, 6:7], ALU.is_ge))
                    k.dvs(v, v.tensor_scalar(r.RC[:, 10:11], r.RC[:, 3:4], -1.0, 1.0, ALU.mult, ALU.add))
                    k.dvs(v, v.tensor_sub(r.RC[:, 12:13], r.RC[:, 8:9], r.RC[:, 9:10]))
                    k.dvs(v, v.tensor_mul(r.RC[:, 13:14], r.RC[:, 12:13], r.RC[:, 10:11]))
                    k.dvs(v, v.tensor_add(r.RC[:, 8:9], r.RC[:, 9:10], r.RC[:, 13:14]))
                    k.dvs(v, v.tensor_sub(r.RC[:, 12:13], r.RC[:, 9:10], r.RC[:, 7:8]))
                    k.dvs(v, v.tensor_mul(r.RC[:, 13:14], r.RC[:, 12:13], r.RC[:, 10:11]))
                    k.dvs(v, v.tensor_add(r.RC[:, 7:8], r.RC[:, 7:8], r.RC[:, 13:14]))
                k.dvs(v, v.tensor_add(r.RC[:, 12:13], r.RC[:, 7:8], r.RC[:, 8:9]))
                k.dvs(v, v.tensor_scalar_mul(r.RC[:, 13:14], r.RC[:, 12:13], 0.5))
                k.dvs(v, v.tensor_copy(r.JMI[:, :], r.RC[:, 13:14]))
                k.dvs(v, v.tensor_copy(r.RC[:, 9:10], r.JMI[:, :]))
                k.dvs(v, v.scalar_tensor_tensor(r.AND[:, :], r.POS[:, :], r.RC[:, 9:10], r.EQ[:, :],
                                                ALU.is_le, ALU.mult, accum_out=r.RC[:, 4:5]))
                DV.inc(v.tensor_copy(r.RCB[:, 0:1], r.RC[:, 4:5]))
            k.dv(j_dv)
            dv_j = DV.n

            def j_pe(t, it=it):
                t.wait_ge(DV.h, dv_j)
                PE.inc(t.matmul(ps_r[:, 3:4], r.ONES2[:, :], r.RCB[:, 0:1],
                                start=True, stop=True))
                pe_cnt2[0] = PE.n
            k.pe(j_pe)

        def j_fin(v):
            v.wait_ge(PE.h, pe_cnt2[0])
            k.dvs(v, v.tensor_tensor(r.RC[:, 3:4], ps_r[:, 3:4], r.RC[:, 6:7], ALU.is_ge))
            k.dvs(v, v.tensor_scalar(r.RC[:, 10:11], r.RC[:, 3:4], -1.0, 1.0, ALU.mult, ALU.add))
            k.dvs(v, v.tensor_sub(r.RC[:, 12:13], r.RC[:, 8:9], r.RC[:, 9:10]))
            k.dvs(v, v.tensor_mul(r.RC[:, 13:14], r.RC[:, 12:13], r.RC[:, 10:11]))
            k.dvs(v, v.tensor_add(r.RC[:, 11:12], r.RC[:, 9:10], r.RC[:, 13:14]))
            k.dvs(v, v.scalar_tensor_tensor(r.AND[:, :], r.POS[:, :], r.RC[:, 11:12], r.EQ[:, :],
                                            ALU.is_le, ALU.mult))
            k.dvs(v, v.tensor_add(r.SEL[:, :], r.GE[:, :], r.AND[:, :]))
            k.dvs(v, v.tensor_copy(r.PRE[:, :], r.SEL[:, :]))
            for sh in (1, 2, 4, 8, 16):
                k.dvs(v, v.tensor_add(r.PR2[:, sh:32], r.PRE[:, sh:32], r.PRE[:, 0:32 - sh]))
                k.dvs(v, v.tensor_copy(r.PR2[:, 0:sh], r.PRE[:, 0:sh]))
                k.dvs(v, v.tensor_copy(r.PRE[:, :], r.PR2[:, :]))
            DV.inc(v.tensor_copy(r.RCB[:, 1:2], r.PRE[:, 31:32]))
        k.dv(j_fin)
        dv_rank = DV.n

        def rank_pe(t):
            t.wait_ge(DV.h, dv_rank)
            PE.inc(t.matmul(ps_r[:, 0:1], r.TRIW[:, :], r.RCB[:, 1:2], start=True, stop=True))
        k.pe(rank_pe)
        pe_rank = PE.n

        def rank_dv(v):
            v.wait_ge(PE.h, pe_rank)
            k.dvs(v, v.tensor_copy(r.RC[:, 14:15], ps_r[:, 0:1]))
            k.dvs(v, v.tensor_single_scalar(r.PR2[:, :], r.PRE[:, :], r.RC[:, 14:15], ALU.add))
            DV.inc(v.tensor_mul(r.RANK[:, :], r.PR2[:, :], r.SEL[:, :]))
        k.dv(rank_dv)

        def vals_dv(v):
            DV.wait(v)
            v.tensor_copy(r.VALS[:, 0:96:3], r.STIDX[:, :])
            v.tensor_copy(r.VALS[:, 1:96:3], r.POS[:, :])
            DV.inc(v.tensor_copy(r.VALS[:, 2:96:3], r.SCT[:, :]))
        k.dv(vals_dv)
        dv_vals = DV.n

        pe_x = [0]
        for tc in range(32):
            def x_dv(v, tc=tc):
                if tc == 0:
                    v.wait_ge(DV.h, dv_vals)
                else:
                    v.wait_ge(PE.h, pe_x[0])
                last = None
                for jc in range(6):
                    jv = r.JF[:, jc * 128:(jc + 1) * 128] if jc < 4 else \
                        r.JOWN[:, (jc - 4) * 128:(jc - 3) * 128]
                    last = v.tensor_scalar(r.PT6[:, jc * 128:(jc + 1) * 128], jv,
                                           r.RANK[:, tc:tc + 1], r.SEL[:, tc:tc + 1],
                                           ALU.is_equal, ALU.mult)
                DV.inc(last)
            k.dv(x_dv)
            dv_x = DV.n

            def x_pe(t, tc=tc):
                t.wait_ge(DV.h, dv_x)
                mmv = None
                for jc in range(6):
                    mmv = t.matmul(ps_rx[:, jc * 3:(jc + 1) * 3],
                                   r.PT6[:, jc * 128:(jc + 1) * 128],
                                   r.VALS[:, 3 * tc:3 * (tc + 1)],
                                   start=(tc == 0 and jc == 0), stop=(tc == 31 and jc == 5))
                PE.inc(mmv)
                pe_x[0] = PE.n
            k.pe(x_pe)

        def x_fin(v):
            v.wait_ge(PE.h, pe_x[0])
            v.tensor_copy(r.IDXS[:, :], ps_rx[:, 0:12])
            v.tensor_copy(r.OWNI[:, :], ps_rx[:, 12:18])
            v.tensor_copy(r.IDXI[:, 0:4], r.IDXS[:, 0:12:3])
            v.tensor_copy(r.IDXI[:, 4:6], r.OWNI[:, 0:6:3])
            DV.inc(v.tensor_copy(r.IDXI[:, 6:8], r.OWNI[:, 1:6:3]))
        k.dv(x_fin)


def rope128(k, v, dst, srcb, cosf, sinf, scr, ncols):
    v.tensor_mul(scr[0:64, :ncols], srcb[64:128, :ncols], sinf[64:128, :ncols])
    v.tensor_mul(dst[0:64, :ncols], srcb[0:64, :ncols], cosf[0:64, :ncols])
    k.dvs(v, v.tensor_sub(dst[0:64, :ncols], dst[0:64, :ncols], scr[0:64, :ncols]))
    v.tensor_mul(scr[64:128, :ncols], srcb[0:64, :ncols], sinf[0:64, :ncols])
    k.dvs(v, v.tensor_mul(dst[64:128, :ncols], srcb[64:128, :ncols], cosf[64:128, :ncols]))
    return v.tensor_add(dst[64:128, :ncols], dst[64:128, :ncols], scr[64:128, :ncols])


def ffn_tm_export(k, io, call):
    nc = k.nc
    LD, LD2, GP, CC, PE, DV, AC = (k.sems[n] for n in ("LD", "LD2", "GP", "CC", "PE", "DV", "AC"))
    with contextlib.ExitStack() as ph:
        TMB = ph.enter_context(nc.sbuf_tensor(f"f_TMB{call}", [128, D], BF16))
        ps_t = ph.enter_context(nc.psum_tensor(f"f_pst{call}", [128, 256], BF16))
        dv_done = [0]
        st_done = [0]
        for tt in range(16):
            for dc in range(NCH):
                def t_pe(t, tt=tt, dc=dc):
                    if tt == 0 and dc == 0:
                        k.fence(t)
                    DV.wait(t)
                    if dc == 0 and tt > 0:
                        LD2.wait(t)
                    PE.inc(t.transpose(ps_t[:, (dc % 2) * 128:(dc % 2) * 128 + 128],
                                       k.XLN[:, dc * TOK + tt * 128: dc * TOK + (tt + 1) * 128],
                                       k.IDENT[:, :]))
                k.pe(t_pe)
                pe_t = PE.n

                def t_dv(v, tt=tt, dc=dc):
                    v.wait_ge(PE.h, pe_t)
                    DV.inc(v.tensor_copy(TMB[:, dc * 128:(dc + 1) * 128],
                                         ps_t[:, (dc % 2) * 128:(dc % 2) * 128 + 128]))
                k.dv(t_dv)
            dv_row = DV.n

            def t_st(e, tt=tt):
                e.wait_ge(DV.h, dv_row)
                LD2.dma(e.dma_start(out=io["xln_tm_own"][tt * 128:(tt + 1) * 128, :],
                                    in_=TMB[:, :]))
            k.sync(t_st)

    def t_cc(g):
        LD2.wait(g)
        CC.inc(g.collective_compute(
            "AllGather", ALU.bypass,
            replica_groups=[[0, 1], [2, 3], [4, 5], [6, 7]],
            ins=[io["xln_tm_own"][:, :]], outs=[io["xln_tm_all"][:, :]]), 1)
    k.gp(t_cc)
    k._f_tm_cc = k.sems["CC"].n


def ffn_body(k, io, call, sfx):
    nc = k.nc
    LD, LD2, GP, CC, PE, DV, AC = (k.sems[n] for n in ("LD", "LD2", "GP", "CC", "PE", "DV", "AC"))
    r = k._r
    SCF = 1.0 / float(np.sqrt(128.0))

    fph = contextlib.ExitStack()
    sbo = lambda n, s, d: fph.enter_context(nc.sbuf_tensor(f"{n}{sfx}", list(s), d))
    XSF = sbo("f_XSF", [128, NCH * 256], BF16)
    AVF = sbo("f_AVF", [128, 16 * 256], BF16)
    X1 = sbo("f_X1", [128, NCH * 256], F32)
    GBC = sbo("f_GBC", [128, 256], BF16)
    OPB = sbo("f_OPB", [128, TOK], F32)
    MB32 = sbo("f_MB32", [128, TOK], F32)
    RB32 = sbo("f_RB32", [128, TOK], F32)
    ROWS = sbo("f_ROWS", [1, TOK], F32)
    ROWB = sbo("f_ROWB", [1, TOK], BF16)
    ROWB2 = sbo("f_ROWB2", [1, TOK], BF16)
    WS1 = sbo("f_WS1", [128, NCH * 128], BF16)
    HF = sbo("f_HF", [128, NCH * 256], BF16)
    SQA = sbo("f_SQA", [128, 16 * 256], BF16)
    LNW = sbo("f_LNW", [128, NCH], F32)
    RR = sbo("f_RR", [128, 8], F32)
    NRMF = sbo("f_NRMF", [1, 8], F32)
    NEGCF = sbo("f_NEGCF", [128, 1], F32)
    NCBF = sbo("f_NCBF", [1, 4], BF16)

    with contextlib.ExitStack() as pa:
        sba = lambda n, s, d: pa.enter_context(nc.sbuf_tensor(f"{n}{sfx}", list(s), d))
        XSB = sba("f_XSB", [128, 2 * D], BF16)
        CSTM = sba("f_CSTM", [128, 2 * 256], BF16)
        COSF = sba("f_COSF", [128, 256], BF16)
        SINF = sba("f_SINF", [128, 256], BF16)

        QF = sba("f_QF", [128, 16 * 256], BF16)
        KFm = sba("f_KFm", [128, 8 * 256], BF16)
        VFm = sba("f_VFm", [128, 8 * 256], BF16)

        MASKF = sba("f_MASKF", [128, 16 * 512], BF16)
        KG = sba("f_KG", [128, 2048], BF16)
        VG = sba("f_VG", [128, 2048], BF16)
        EXF = sba("f_EXF", [128, 2 * 512], BF16)
        SMS = sba("f_SMS", [128, 2 * 512], F32)
        DENF = sba("f_DENF", [128, 512], F32)
        OAVR = sba("f_OAVR", [128, 512], BF16)
        RSC = sba("f_RSC", [128, 256], BF16)

        BQF = sba("f_BQF", [128, 16], F32)
        BKF = sba("f_BKF", [128, 8], F32)
        BVF = sba("f_BVF", [128, 8], F32)


        QFLB = sba("f_QFLB", [128, 256], F32)




        ps_a = fph.enter_context(nc.psum_tensor(f"f_psa{sfx}", [128, 2048], F32))
        ps_b = fph.enter_context(nc.psum_tensor(f"f_psb{sfx}", [128, 512], F32))
        ps_row = fph.enter_context(nc.psum_tensor(f"f_psrow{sfx}", [128, 512], F32))
        ps_tt = fph.enter_context(nc.psum_tensor(f"f_pstt{sfx}", [128, 256], BF16))

        def f2_gp(g):
            k.fence(g)
            g.wait_ge(CC.h, k._f_tm_cc)
            for jc in range(2):
                GP.dma(g.indirect_dma_start(
                    out=XSB[:, jc * D:(jc + 1) * D], out_offset=None,
                    in_=io["xln_tm_all"][:, :],
                    in_offset=IndirectOffsetOnAxis(r.IDXI[:, 4 + jc:5 + jc], 0)))
                GP.dma(g.indirect_dma_start(
                    out=CSTM[:, jc * 256:(jc + 1) * 256], out_offset=None,
                    in_=io["cos_f_tbl"][:, :],
                    in_offset=IndirectOffsetOnAxis(r.IDXI[:, 6 + jc:7 + jc], 0)))
        k.gp(f2_gp)
        gp_gath = GP.n

        def bc_sync(e):
            k.fence(e)
            LD.dma(e.dma_start(out=ROWS[0:1, :], in_=io["ownpos"][0:1, :]))
            LD.dma(e.dma_start(out=QFLB[0:1, 0:256], in_=io["qflat"][0:1, :]))
            LD.dma(e.dma_start(out=LNW[:, :], in_=io["ln1w"][:, :]))
            LD.dma(e.dma_start(out=BQF[:, :], in_=io["bq_f"][:, :]))
            LD.dma(e.dma_start(out=BKF[:, :], in_=io["bk_f"][:, :]))
            LD.dma(e.dma_start(out=BVF[:, :], in_=io["bv_f"][:, :]))
        k.sync(bc_sync)
        ld_rows = LD.n

        def bc_dv(v):
            v.wait_ge(LD.h, ld_rows)
            k.dvs(v, v.tensor_copy(ROWB[0:1, :], ROWS[0:1, :]))
            k.dvs(v, v.tensor_sub(ROWS[0:1, :], ROWS[0:1, :], ROWB[0:1, :]))
            DV.inc(v.tensor_copy(ROWB2[0:1, :], ROWS[0:1, :]))
        k.dv(bc_dv)
        dv_rows = DV.n

        def bc_pe(t):
            t.wait_ge(DV.h, dv_rows)
            mmv = None
            for (o, w) in mm_groups(TOK):
                t.matmul(ps_a[:, o:o + w], k.ONER[0:1, :], ROWB[0:1, o:o + w], start=True, stop=False)
                mmv = t.matmul(ps_a[:, o:o + w], k.ONER[0:1, :], ROWB2[0:1, o:o + w], start=False, stop=True)
            PE.inc(mmv)
        k.pe(bc_pe)
        pe_opb = PE.n

        def bc_dv2(v):
            v.wait_ge(PE.h, pe_opb)
            k.dvs(v, v.tensor_copy(OPB[:, :], ps_a[:, :]))
            k.dvs(v, v.tensor_copy(ROWB[0:1, 0:256], QFLB[0:1, 0:256]))
            k.dvs(v, v.tensor_sub(ROWS[0:1, 0:256], QFLB[0:1, 0:256], ROWB[0:1, 0:256]))
            DV.inc(v.tensor_copy(ROWB2[0:1, 0:256], ROWS[0:1, 0:256]))
        k.dv(bc_dv2)
        dv_qf = DV.n

        def bc_pe2(t):
            t.wait_ge(DV.h, dv_qf)
            t.matmul(ps_b[:, 0:256], k.ONER[0:1, :], ROWB[0:1, 0:256], start=True, stop=False)
            PE.inc(t.matmul(ps_b[:, 0:256], k.ONER[0:1, :], ROWB2[0:1, 0:256], start=False, stop=True))
        k.pe(bc_pe2)
        pe_qf = PE.n

        def bc_dv3(v):
            v.wait_ge(PE.h, pe_qf)
            DV.inc(v.tensor_copy(QFLB[:, :], ps_b[:, 0:256]))
        k.dv(bc_dv3)
        dv_qfb = DV.n

        def mk_gp(g):
            GP.inc(g.iota(RR[:, 0:1], pattern=[[0, 1]], base=0, channel_multiplier=1,
                          allow_small_or_imprecise_dtypes=True), 1)
        k.gp(mk_gp)
        gp_iota = GP.n

        def mk_dv(v):
            v.wait_ge(GP.h, gp_iota)
            v.wait_ge(DV.h, dv_qfb)
            for kc in range(16):
                k.dvs(v, v.tensor_scalar(RR[:, 2:3], RR[:, 0:1], 1.0, float(128 * kc),
                                         ALU.mult, ALU.add))
                k.dvs(v, v.tensor_scalar(SQA[:, 0:256], QFLB[:, :], RR[:, 2:3], -1.0,
                                         ALU.is_ge, ALU.add))
                for qh2 in range(2):
                    k.dvs(v, v.tensor_scalar(MASKF[:, kc * 512 + qh2 * 256: kc * 512 + (qh2 + 1) * 256],
                                             SQA[:, 0:256], 30000.0, None, ALU.mult))
            DV.inc(v.tensor_copy(RR[:, 1:2], RR[:, 0:1]))
        k.dv(mk_dv)

        def f3_ac(s):
            s.wait_ge(GP.h, gp_gath)
            DV.wait(s)
            AC.inc(s.activation(SQA[:, 0:2 * D], XSB[:, :], AF.Square))
        k.ac(f3_ac)
        ac_sq1 = AC.n

        def f3_dv(v):
            v.wait_ge(AC.h, ac_sq1)
            for jc in range(2):
                k.dvs(v, v.reduce_sum(RR[:, jc:jc + 1], SQA[:, jc * D:(jc + 1) * D], axis=AX.X))
            k.dvs(v, v.tensor_scalar(RR[:, 2:4], RR[:, 0:2], 1.0 / D, EPS_RMS, ALU.mult, ALU.add))
            DV.inc(v.tensor_copy(RR[:, 4:5], RR[:, 2:3]))
        k.dv(f3_dv)
        dv_rms1 = DV.n

        def f3_ac2(s):
            s.wait_ge(DV.h, dv_rms1)
            AC.inc(s.activation(RR[:, 6:8], RR[:, 2:4], AF.Sqrt))
        k.ac(f3_ac2)
        ac_rms1 = AC.n

        def f3_dv2(v):
            v.wait_ge(AC.h, ac_rms1)
            k.dvs(v, v.reciprocal(RR[:, 4:6], RR[:, 6:8]))
            for jc in range(2):
                k.dvs(v, v.tensor_scalar(SQA[:, jc * D:(jc + 1) * D], XSB[:, jc * D:(jc + 1) * D],
                                         RR[:, 4 + jc:5 + jc], None, ALU.mult))
            DV.inc(v.tensor_copy(RR[:, 0:1], RR[:, 4:5]))
        k.dv(f3_dv2)
        dv_hs = DV.n

        pe_tp = [0]
        for jc in range(2):
            for dc in range(NCH):
                def tp_pe(t, jc=jc, dc=dc):
                    t.wait_ge(DV.h, dv_hs)
                    DV.wait(t)
                    t.transpose(ps_tt[:, 0:128], XSB[:, jc * D + dc * 128: jc * D + (dc + 1) * 128],
                                k.IDENT[:, :])
                    PE.inc(t.transpose(ps_tt[:, 128:256], SQA[:, jc * D + dc * 128: jc * D + (dc + 1) * 128],
                                       k.IDENT[:, :]))
                    pe_tp[0] = PE.n
                k.pe(tp_pe)

                def tp_dv(v, jc=jc, dc=dc):
                    v.wait_ge(PE.h, pe_tp[0])
                    v.tensor_copy(XSF[:, dc * 256 + jc * 128: dc * 256 + jc * 128 + 128],
                                  ps_tt[:, 0:128])
                    DV.inc(v.tensor_scalar(HF[:, dc * 256 + jc * 128: dc * 256 + jc * 128 + 128],
                                           ps_tt[:, 128:256], LNW[:, dc:dc + 1], None, ALU.mult))
                k.dv(tp_dv)
        dv_hf = DV.n

        for jc in range(2):
            def cs_pe(t, jc=jc):
                t.wait_ge(GP.h, gp_gath)
                DV.wait(t)
                t.transpose(ps_tt[:, 0:128], CSTM[:, jc * 256: jc * 256 + 128], k.IDENT[:, :])
                PE.inc(t.transpose(ps_tt[:, 128:256], CSTM[:, jc * 256 + 128: (jc + 1) * 256],
                                   k.IDENT[:, :]))
            k.pe(cs_pe)
            pe_cs = PE.n

            def cs_dv(v, jc=jc):
                v.wait_ge(PE.h, pe_cs)
                v.tensor_copy(COSF[:, jc * 128:(jc + 1) * 128], ps_tt[:, 0:128])
                DV.inc(v.tensor_copy(SINF[:, jc * 128:(jc + 1) * 128], ps_tt[:, 128:256]))
            k.dv(cs_dv)
        dv_cs = DV.n

        def qkv_ld(e):
            e.wait_ge(GP.h, k.cast_done["wq_f"])
            e.wait_ge(GP.h, k.cast_done["wk_f"])
            e.wait_ge(GP.h, k.cast_done["wv_f"])
        k.sync(qkv_ld)

        for kind, nmc, wname, dst, bias in (("q", 16, "wq_f_bf", QF, BQF),
                                            ("k", 8, "wk_f_bf", KFm, BKF),
                                            ("v", 8, "wv_f_bf", VFm, BVF)):
            for Mc in range(nmc):
                def qk_ld(e, wname=wname, Mc=Mc):
                    PE.wait(e)
                    LD.dma(e.dma_start(out=WS1[:, :], in_=io[wname][Mc, :, :]))
                k.sync(qk_ld)
                ld_w = LD.n

                def qk_pe(t, Mc=Mc):
                    t.wait_ge(LD.h, ld_w)
                    t.wait_ge(DV.h, dv_hf)
                    DV.wait(t)
                    mmv = None
                    for dc in range(NCH):
                        mmv = t.matmul(ps_b[:, 0:256], WS1[:, dc * 128:(dc + 1) * 128],
                                       HF[:, dc * 256:(dc + 1) * 256],
                                       start=(dc == 0), stop=(dc == NCH - 1))
                    PE.inc(mmv)
                k.pe(qk_pe)
                pe_qk = PE.n

                def qk_dv(v, kind=kind, Mc=Mc, dst=dst, bias=bias):
                    v.wait_ge(PE.h, pe_qk)
                    v.wait_ge(DV.h, dv_cs)
                    if kind == "v":
                        DV.inc(v.tensor_scalar(dst[:, Mc * 256:(Mc + 1) * 256], ps_b[:, 0:256],
                                               bias[:, Mc:Mc + 1], None, ALU.add))
                    else:
                        k.dvs(v, v.tensor_scalar(RSC[:, :], ps_b[:, 0:256],
                                                 bias[:, Mc:Mc + 1], None, ALU.add))
                        DV.inc(rope128(k, v, dst[:, Mc * 256:(Mc + 1) * 256],
                                       RSC[:, :], COSF, SINF, SQA[:, 0:256], 256))
                k.dv(qk_dv)
        dv_qkv = DV.n
        k.tap("xsb" + sfx, XSB[:, :], [128, 2 * D], BF16)
        k.tap("hf" + sfx, HF[:, :], [128, NCH * 256], BF16)
        k.tap("qff" + sfx, QF[:, :], [128, 16 * 256], BF16)
        k.tap("kff" + sfx, KFm[:, :], [128, 8 * 256], BF16)

        def f6_st(e):
            e.wait_ge(DV.h, dv_qkv)
            for Mc in range(8):
                LD2.dma(e.dma_start(out=io["kf_own"][Mc * 128:(Mc + 1) * 128, :],
                                    in_=KFm[:, Mc * 256:(Mc + 1) * 256]))
        k.sync(f6_st)

        pe_vt = [0]
        for Mc in range(8):
            for jc in range(2):
                def v_pe(t, Mc=Mc, jc=jc):
                    t.wait_ge(DV.h, dv_qkv)
                    DV.wait(t)
                    PE.inc(t.transpose(ps_tt[:, (jc % 2) * 128:(jc % 2) * 128 + 128],
                                       VFm[:, Mc * 256 + jc * 128: Mc * 256 + (jc + 1) * 128],
                                       k.IDENT[:, :]))
                    pe_vt[0] = PE.n
                k.pe(v_pe)

                def v_dv(v, Mc=Mc, jc=jc):
                    v.wait_ge(PE.h, pe_vt[0])
                    DV.inc(v.tensor_copy(SQA[:, jc * 1024 + Mc * 128: jc * 1024 + (Mc + 1) * 128],
                                         ps_tt[:, (jc % 2) * 128:(jc % 2) * 128 + 128]))
                k.dv(v_dv)
        dv_vtm = DV.n

        def f6_st2(e):
            e.wait_ge(DV.h, dv_vtm)
            for jc in range(2):
                LD2.dma(e.dma_start(out=io["vtm_own"][jc * 128:(jc + 1) * 128, :],
                                    in_=SQA[:, jc * 1024:(jc + 1) * 1024]))
        k.sync(f6_st2)

        def f6_cc(g):
            LD2.wait(g)
            CC.inc(g.collective_compute(
                "AllGather", ALU.bypass, replica_groups=[list(range(8))],
                ins=[io["kf_own"][:, :]], outs=[io["kf_all"][:, :]]), 1)
            CC.inc(g.collective_compute(
                "AllGather", ALU.bypass, replica_groups=[list(range(8))],
                ins=[io["vtm_own"][:, :]], outs=[io["vtm_all"][:, :]]), 1)
        k.gp(f6_cc)
        k._f_kv_cc = CC.n

        def n_ac2(s):
            s.wait_ge(DV.h, dv_qkv)
            LD2.wait(s)
            AC.inc(s.activation(SQA[:, 0:16 * 256], QF[:, :], AF.Square))
        k.ac(n_ac2)
        ac_nq = AC.n

        pe_nn = [0]
        for Mc in range(16):
            def nq_pe(t, Mc=Mc):
                t.wait_ge(AC.h, ac_nq)
                DV.wait(t)
                PE.inc(t.matmul(ps_row[0:1, 0:256], k.ONEC[:, :], SQA[:, Mc * 256:(Mc + 1) * 256],
                                start=True, stop=True))
                pe_nn[0] = PE.n
            k.pe(nq_pe)

            def nq_dv(v, Mc=Mc):
                v.wait_ge(PE.h, pe_nn[0])
                k.dvs(v, v.reduce_max(NRMF[0:1, 1:2], ps_row[0:1, 0:256], axis=AX.X))
                if Mc == 0:
                    DV.inc(v.tensor_copy(NRMF[0:1, 0:1], NRMF[0:1, 1:2]))
                else:
                    k.dvs(v, v.tensor_tensor(NRMF[0:1, 2:3], NRMF[0:1, 0:1], NRMF[0:1, 1:2], ALU.max))
                    DV.inc(v.tensor_copy(NRMF[0:1, 0:1], NRMF[0:1, 2:3]))
            k.dv(nq_dv)

        def nk_ac(s):
            DV.wait(s)
            AC.inc(s.activation(SQA[:, 0:8 * 256], KFm[:, :], AF.Square))
        k.ac(nk_ac)
        ac_nk = AC.n

        for Mc in range(8):
            def nk2_pe(t, Mc=Mc):
                t.wait_ge(AC.h, ac_nk)
                DV.wait(t)
                PE.inc(t.matmul(ps_row[0:1, 0:256], k.ONEC[:, :], SQA[:, Mc * 256:(Mc + 1) * 256],
                                start=True, stop=True))
                pe_nn[0] = PE.n
            k.pe(nk2_pe)

            def nk2_dv(v, Mc=Mc):
                v.wait_ge(PE.h, pe_nn[0])
                k.dvs(v, v.reduce_max(NRMF[0:1, 1:2], ps_row[0:1, 0:256], axis=AX.X))
                if Mc == 0:
                    DV.inc(v.tensor_copy(NRMF[0:1, 3:4], NRMF[0:1, 1:2]))
                else:
                    k.dvs(v, v.tensor_tensor(NRMF[0:1, 2:3], NRMF[0:1, 3:4], NRMF[0:1, 1:2], ALU.max))
                    DV.inc(v.tensor_copy(NRMF[0:1, 3:4], NRMF[0:1, 2:3]))
            k.dv(nk2_dv)

        def c_dvf(v):
            DV.wait(v)
            k.dvs(v, v.tensor_mul(NRMF[0:1, 4:5], NRMF[0:1, 0:1], NRMF[0:1, 3:4]))
            DV.inc(v.tensor_copy(NRMF[0:1, 5:6], NRMF[0:1, 4:5]))
        k.dv(c_dvf)

        def c_acf(s):
            DV.wait(s)
            AC.inc(s.activation(NRMF[0:1, 6:7], NRMF[0:1, 5:6], AF.Sqrt, scale=1.0 / 128.0))
        k.ac(c_acf)
        ac_cf = AC.n

        def c_dvf2(v):
            v.wait_ge(AC.h, ac_cf)
            k.dvs(v, v.tensor_scalar(NRMF[0:1, 7:8], NRMF[0:1, 6:7], -1.0, -1.0, ALU.mult, ALU.add))
            DV.inc(v.tensor_copy(NCBF[0:1, 0:1], NRMF[0:1, 7:8]))
        k.dv(c_dvf2)
        dv_ncb = DV.n

        def c_pef(t):
            t.wait_ge(DV.h, dv_ncb)
            PE.inc(t.matmul(ps_row[:, 0:1], k.ONER[0:1, :], NCBF[0:1, 0:1], start=True, stop=True))
        k.pe(c_pef)
        pe_ncb = PE.n

        def c_dvf3(v):
            v.wait_ge(PE.h, pe_ncb)
            DV.inc(v.tensor_copy(NEGCF[:, :], ps_row[:, 0:1]))
        k.dv(c_dvf3)
        dv_negcf = DV.n

        for g in range(8):
            def a_ld(e, g=g):
                if g == 0:
                    e.wait_ge(CC.h, k._f_kv_cc)
                PE.wait(e)
                for rr in range(8):
                    LD.dma(e.dma_start(out=KG[:, rr * 256:(rr + 1) * 256],
                                       in_=io["kf_all"][rr * 1024 + g * 128: rr * 1024 + (g + 1) * 128, :]))
                for kc in range(16):
                    LD.dma(e.dma_start(out=VG[:, kc * 128:(kc + 1) * 128],
                                       in_=io["vtm_all"][kc * 128:(kc + 1) * 128,
                                                         g * 128:(g + 1) * 128]))
            k.sync(a_ld)
            ld_kv = LD.n

            def a_init(v, g=g):
                v.wait_ge(DV.h, dv_negcf)
                PE.wait(v)
                DV.inc(v.memset(DENF[:, :], 0.0))
            k.dv(a_init)

            ac_ex = {}
            dv_dn = {}
            pe_av8 = {}
            for kc in range(16):
                par = kc % 2

                def s_pe(t, g=g, kc=kc, par=par):
                    t.wait_ge(LD.h, ld_kv)
                    if kc < 2:
                        DV.wait(t)
                    else:
                        t.wait_ge(AC.h, ac_ex[kc - 2])
                    PE.inc(t.matmul(ps_a[:, par * 512:(par + 1) * 512],
                                    KG[:, kc * 128:(kc + 1) * 128],
                                    QF[:, 2 * g * 256:(2 * g + 2) * 256],
                                    start=True, stop=True))
                k.pe(s_pe)
                pe_s8 = PE.n

                def m_dv(v, g=g, kc=kc, par=par):
                    v.wait_ge(PE.h, pe_s8)
                    if kc >= 2:
                        v.wait_ge(AC.h, ac_ex[kc - 2])
                    DV.inc(v.tensor_add(SMS[:, par * 512:(par + 1) * 512],
                                        ps_a[:, par * 512:(par + 1) * 512],
                                        MASKF[:, kc * 512:(kc + 1) * 512]))
                k.dv(m_dv)
                dv_m = DV.n

                def e_ac(s, g=g, kc=kc, par=par):
                    s.wait_ge(DV.h, dv_m)
                    if kc >= 2:
                        s.wait_ge(DV.h, dv_dn[kc - 2])
                        s.wait_ge(PE.h, pe_av8[kc - 2])
                    AC.inc(s.activation(EXF[:, par * 512:(par + 1) * 512],
                                        SMS[:, par * 512:(par + 1) * 512], AF.Exp,
                                        bias=NEGCF[:, 0:1], scale=SCF))
                    ac_ex[kc] = AC.n
                k.ac(e_ac)

                def d_dv(v, g=g, kc=kc, par=par):
                    v.wait_ge(AC.h, ac_ex[kc])
                    DV.inc(v.tensor_add(DENF[:, :], DENF[:, :], EXF[:, par * 512:(par + 1) * 512]))
                    dv_dn[kc] = DV.n
                k.dv(d_dv)

                def av8_pe(t, g=g, kc=kc, par=par):
                    t.wait_ge(AC.h, ac_ex[kc])
                    PE.inc(t.matmul(ps_b[:, 0:512], VG[:, kc * 128:(kc + 1) * 128],
                                    EXF[:, par * 512:(par + 1) * 512],
                                    start=(kc == 0), stop=(kc == 15)))
                    pe_av8[kc] = PE.n
                k.pe(av8_pe)

            def dn_pe8(t, g=g):
                t.wait_ge(DV.h, dv_dn[15])
                PE.inc(t.matmul(ps_row[0:1, 0:512], k.ONEC32[:, :], DENF[:, :],
                                start=True, stop=True))
            k.pe(dn_pe8)
            pe_dn8 = PE.n

            def rc_dv8(v, g=g):
                v.wait_ge(PE.h, pe_dn8)
                k.dvs(v, v.reciprocal(DENF[0:1, 0:512], ps_row[0:1, 0:512]))
                k.dvs(v, v.tensor_copy(ROWB[0:1, 0:512], DENF[0:1, 0:512]))
                k.dvs(v, v.tensor_sub(DENF[0:1, 0:512], DENF[0:1, 0:512], ROWB[0:1, 0:512]))
                k.dvs(v, v.tensor_copy(ROWB2[0:1, 0:512], DENF[0:1, 0:512]))
                v.wait_ge(PE.h, pe_av8[15])
                DV.inc(v.tensor_copy(OAVR[:, :], ps_b[:, 0:512]))
            k.dv(rc_dv8)
            dv_rc8 = DV.n

            def bc_pe8(t, g=g):
                t.wait_ge(DV.h, dv_rc8)
                t.wait_ge(AC.h, ac_ex[15])
                t.matmul(ps_a[:, 0:512], k.ONER[0:1, :], ROWB[0:1, 0:512], start=True, stop=False)
                PE.inc(t.matmul(ps_a[:, 0:512], k.ONER[0:1, :], ROWB2[0:1, 0:512],
                                start=False, stop=True))
            k.pe(bc_pe8)
            pe_bc8 = PE.n

            def nm_dv8(v, g=g):
                v.wait_ge(PE.h, pe_bc8)
                DV.inc(v.tensor_mul(AVF[:, 2 * g * 256:(2 * g + 2) * 256], OAVR[:, :],
                                    ps_a[:, 0:512]))
            k.dv(nm_dv8)
        dv_att8 = DV.n
        k.tap("avf" + sfx, AVF[:, :], [128, 16 * 256], BF16)

        for dc in range(NCH):
            def w_ld(e, dc=dc):
                if dc == 0:
                    e.wait_ge(GP.h, k.cast_done["wo_f"])
                PE.wait(e)
                LD.dma(e.dma_start(out=WS1[:, 0:D], in_=io["wo_f_bf"][dc, :, :]))
            k.sync(w_ld)
            ld_wo9 = LD.n

            def w9_pe(t, dc=dc, ld_wo9=ld_wo9):
                t.wait_ge(LD.h, ld_wo9)
                t.wait_ge(DV.h, dv_att8)
                DV.wait(t)
                mmv = None
                for dvc in range(16):
                    mmv = t.matmul(ps_b[:, 0:256], WS1[:, dvc * 128:(dvc + 1) * 128],
                                   AVF[:, dvc * 256:(dvc + 1) * 256],
                                   start=(dvc == 0), stop=(dvc == 15))
                PE.inc(mmv)
            k.pe(w9_pe)
            pe_w9 = PE.n

            def w9_dv(v, dc=dc):
                v.wait_ge(PE.h, pe_w9)
                DV.inc(v.tensor_add(X1[:, dc * 256:(dc + 1) * 256], ps_b[:, 0:256],
                                    XSF[:, dc * 256:(dc + 1) * 256]))
            k.dv(w9_dv)
        dv_x1 = DV.n
        k.tap("x1" + sfx, X1[:, :], [128, NCH * 256], F32)

        def r2_ld(e):
            LD.dma(e.dma_start(out=LNW[:, :], in_=io["ln2w"][:, :]))
        k.sync(r2_ld)
        ld_ln2 = LD.n

        def r2_ac(s):
            s.wait_ge(DV.h, dv_x1)
            LD2.wait(s)
            AC.inc(s.activation(SQA[:, 0:NCH * 256], X1[:, :], AF.Square))
        k.ac(r2_ac)
        ac_r2 = AC.n

        def r2_pe(t):
            t.wait_ge(AC.h, ac_r2)
            DV.wait(t)
            mmv = None
            for dc in range(NCH):
                mmv = t.matmul(ps_row[0:1, 0:256], k.ONEC[:, :], SQA[:, dc * 256:(dc + 1) * 256],
                               start=(dc == 0), stop=(dc == NCH - 1))
            PE.inc(mmv)
        k.pe(r2_pe)
        pe_r2 = PE.n

        def r2_dv(v):
            v.wait_ge(PE.h, pe_r2)
            k.dvs(v, v.tensor_scalar(ROWS[0:1, 0:256], ps_row[0:1, 0:256], 1.0 / D, EPS_RMS,
                                     ALU.mult, ALU.add))
            DV.inc(v.tensor_copy(ROWS[0:1, 256:257], ROWS[0:1, 0:1]))
        k.dv(r2_dv)
        dv_r2 = DV.n

        def r2_ac2(s):
            s.wait_ge(DV.h, dv_r2)
            AC.inc(s.activation(ROWS[0:1, 512:768], ROWS[0:1, 0:256], AF.Sqrt))
        k.ac(r2_ac2)
        ac_r2b = AC.n

        def r2_dv2(v):
            v.wait_ge(AC.h, ac_r2b)
            k.dvs(v, v.reciprocal(ROWS[0:1, 0:256], ROWS[0:1, 512:768]))
            k.dvs(v, v.tensor_copy(ROWB[0:1, 0:256], ROWS[0:1, 0:256]))
            k.dvs(v, v.tensor_sub(ROWS[0:1, 256:512], ROWS[0:1, 0:256], ROWB[0:1, 0:256]))
            DV.inc(v.tensor_copy(ROWB2[0:1, 0:256], ROWS[0:1, 256:512]))
        k.dv(r2_dv2)
        dv_r2c = DV.n

        def r2_pe2(t):
            t.wait_ge(DV.h, dv_r2c)
            t.matmul(ps_row[:, 0:256], k.ONER[0:1, :], ROWB[0:1, 0:256], start=True, stop=False)
            PE.inc(t.matmul(ps_row[:, 0:256], k.ONER[0:1, :], ROWB2[0:1, 0:256],
                            start=False, stop=True))
        k.pe(r2_pe2)
        pe_r2b = PE.n

        def r2_dv3(v):
            v.wait_ge(PE.h, pe_r2b)
            v.wait_ge(LD.h, ld_ln2)
            last = None
            for dc in range(NCH):
                last = v.scalar_tensor_tensor(HF[:, dc * 256:(dc + 1) * 256],
                                              X1[:, dc * 256:(dc + 1) * 256],
                                              LNW[:, dc:dc + 1], ps_row[:, 0:256],
                                              ALU.mult, ALU.mult)
            DV.inc(last)
        k.dv(r2_dv3)
        dv_h2 = DV.n
        k.tap("h2f" + sfx, HF[:, :], [128, NCH * 256], BF16)

        pa.close()
        pb = contextlib.ExitStack()
        MS = pb.enter_context(nc.sbuf_tensor(f"f_MS{sfx}", [128, NICH * 256], BF16))
        GS = pb.enter_context(nc.sbuf_tensor(f"f_GS{sfx}", [128, 256], BF16))
        WS2 = pb.enter_context(nc.sbuf_tensor(f"f_WS2{sfx}", [128, NCH * 128], BF16))
        for ic in range(NICH):
            def g_ld(e, ic=ic):
                if ic == 0:
                    e.wait_ge(GP.h, k.cast_done["wu_f"])
                PE.wait(e)
                LD.dma(e.dma_start(out=WS1[:, :], in_=io["wg_f_bf"][ic, :, :]))
                LD.dma(e.dma_start(out=WS2[:, :], in_=io["wu_f_bf"][ic, :, :]))
            k.sync(g_ld)
            ld_g = LD.n

            def g_pe(t, ic=ic):
                t.wait_ge(LD.h, ld_g)
                t.wait_ge(DV.h, dv_h2)
                DV.wait(t)
                AC.wait(t)
                mmv = None
                for dc in range(NCH):
                    t.matmul(ps_b[:, 0:256], WS1[:, dc * 128:(dc + 1) * 128],
                             HF[:, dc * 256:(dc + 1) * 256], start=(dc == 0), stop=(dc == NCH - 1))
                    mmv = t.matmul(ps_row[:, 0:256], WS2[:, dc * 128:(dc + 1) * 128],
                                   HF[:, dc * 256:(dc + 1) * 256], start=(dc == 0), stop=(dc == NCH - 1))
                PE.inc(mmv)
            k.pe(g_pe)
            pe_g = PE.n

            def g_ac(s, ic=ic):
                s.wait_ge(PE.h, pe_g)
                AC.inc(s.activation(GS[:, :], ps_b[:, 0:256], AF.Silu))
            k.ac(g_ac)
            ac_g = AC.n

            def g_dv(v, ic=ic):
                v.wait_ge(AC.h, ac_g)
                DV.inc(v.tensor_mul(MS[:, ic * 256:(ic + 1) * 256], GS[:, :], ps_row[:, 0:256]))
            k.dv(g_dv)
        dv_ms = DV.n
        k.tap("msf" + sfx, MS[:, :], [128, NICH * 256], BF16)

        for half in range(2):
            for ic in range(NICH):
                def d_ld(e, ic=ic, half=half):
                    if ic == 0 and half == 0:
                        e.wait_ge(GP.h, k.cast_done["wd_f"])
                    PE.wait(e)
                    LD.dma(e.dma_start(out=WS1[:, 0:8 * 128], in_=io["wd_f_bf"][ic, :, half * 1024:(half + 1) * 1024]))
                k.sync(d_ld)
                ld_d = LD.n

                def d_pe(t, ic=ic, half=half):
                    t.wait_ge(LD.h, ld_d)
                    t.wait_ge(DV.h, dv_ms)
                    if ic == 0:
                        DV.wait(t)
                    mmv = None
                    for dci in range(8):
                        mmv = t.matmul(ps_a[:, dci * 256:(dci + 1) * 256],
                                       WS1[:, dci * 128:(dci + 1) * 128],
                                       MS[:, ic * 256:(ic + 1) * 256],
                                       start=(ic == 0 and dci % 2 == 0),
                                       stop=(ic == NICH - 1))
                    PE.inc(mmv)
                k.pe(d_pe)
            pe_dn = PE.n

            def d_dv(v, half=half):
                v.wait_ge(PE.h, pe_dn)
                last = None
                for dci in range(8):
                    dc = half * 8 + dci
                    last = v.tensor_add(X1[:, dc * 256:(dc + 1) * 256],
                                        X1[:, dc * 256:(dc + 1) * 256],
                                        ps_a[:, dci * 256:(dci + 1) * 256])
                DV.inc(last)
            k.dv(d_dv)
        dv_proc = DV.n
        k.tap("proc" + sfx, X1[:, :], [128, NCH * 256], F32)

        def gt_dv0(v):
            DV.wait(v)
            v.tensor_copy(GS[:, 0:1], r.OWNI[:, 2:3])
            DV.inc(v.tensor_copy(GS[:, 1:2], r.OWNI[:, 5:6]))
        k.dv(gt_dv0)
        dv_gt0 = DV.n

        def gt_pe(t):
            t.wait_ge(DV.h, dv_gt0)
            t.matmul(ps_row[0:1, 0:128], GS[:, 0:1], k.IDENT[:, :], start=True, stop=True)
            PE.inc(t.matmul(ps_row[0:1, 128:256], GS[:, 1:2], k.IDENT[:, :], start=True, stop=True))
        k.pe(gt_pe)
        pe_gt = PE.n

        def gt_ac(s):
            s.wait_ge(PE.h, pe_gt)
            AC.inc(s.activation(ROWB[0:1, 0:256], ps_row[0:1, 0:256], AF.Sigmoid))
        k.ac(gt_ac)
        ac_gt = AC.n

        def gt_pe2(t):
            t.wait_ge(AC.h, ac_gt)
            PE.inc(t.matmul(ps_row[:, 0:256], k.ONER[0:1, :], ROWB[0:1, 0:256],
                            start=True, stop=True))
        k.pe(gt_pe2)
        pe_gt2 = PE.n

        def gt_dv(v):
            v.wait_ge(PE.h, pe_gt2)
            DV.inc(v.tensor_copy(GBC[:, :], ps_row[:, 0:256]))
        k.dv(gt_dv)
        dv_gbc = DV.n
        k.tap("gbc" + sfx, GBC[:, :], [128, 256], BF16)

        def c_dv(v):
            v.wait_ge(DV.h, dv_gbc)
            for dc in range(NCH):
                k.dvs(v, v.tensor_sub(SQA[:, 0:256], X1[:, dc * 256:(dc + 1) * 256],
                                      XSF[:, dc * 256:(dc + 1) * 256]))
                k.dvs(v, v.tensor_mul(HF[:, dc * 256:(dc + 1) * 256],
                                      SQA[:, 0:256], GBC[:, :]))
            DV.inc(v.tensor_copy(GS[:, 0:1], GBC[:, 0:1]))
        k.dv(c_dv)
        dv_c = DV.n

        k.tap("cfm" + sfx, HF[:, :], [128, NCH * 256], BF16)
        for jc in range(2):
            for dc in range(NCH):
                def ct_pe(t, jc=jc, dc=dc):
                    t.wait_ge(DV.h, dv_c)
                    if dc == 0:
                        LD2.wait(t)
                    DV.wait(t)
                    PE.inc(t.transpose(ps_tt[:, 0:128],
                                       HF[:, dc * 256 + jc * 128: dc * 256 + jc * 128 + 128],
                                       k.IDENT[:, :]))
                k.pe(ct_pe)
                pe_ct = PE.n

                def ct_dv(v, jc=jc, dc=dc):
                    v.wait_ge(PE.h, pe_ct)
                    DV.inc(v.tensor_copy(MS[:, dc * 128:(dc + 1) * 128], ps_tt[:, 0:128]))
                k.dv(ct_dv)
            dv_ct = DV.n

            def ct_st(e, jc=jc):
                e.wait_ge(DV.h, dv_ct)
                LD2.dma(e.dma_start(out=io["cex_own"][jc * 128:(jc + 1) * 128, :],
                                    in_=MS[:, 0:D]))
            k.sync(ct_st)

        def c_cc(g):
            LD2.wait(g)
            CC.inc(g.collective_compute(
                "AllGather", ALU.bypass,
                replica_groups=[[0, 1], [2, 3], [4, 5], [6, 7]],
                ins=[io["cex_own"][:, :]], outs=[io["cex_all"][:, :]]), 1)
        k.gp(c_cc)
        cc_cex = CC.n
        pb.close()

        PRW = fph.enter_context(nc.sbuf_tensor(f"f_PRW{sfx}", [128, 4 * TOK], BF16))

        def pr_dv(v):
            k.fence(v)
            last = None
            for jc in range(4):
                last = v.tensor_scalar(PRW[:, jc * TOK:(jc + 1) * TOK], OPB[:, :],
                                       r.IDXS[:, 3 * jc + 1:3 * jc + 2], None, ALU.is_equal)
            DV.inc(last)
        k.dv(pr_dv)
        dv_prw = DV.n

        def mr_dv(v):
            DV.wait(v)
            k.dvs(v, v.tensor_copy(ROWB[0:1, :], k.STM[0:1, :]))
            k.dvs(v, v.tensor_sub(ROWS[0:1, :], k.STM[0:1, :], ROWB[0:1, :]))
            DV.inc(v.tensor_copy(ROWB2[0:1, :], ROWS[0:1, :]))
        k.dv(mr_dv)
        dv_mr = DV.n

        def mr_pe(t):
            t.wait_ge(DV.h, dv_mr)
            mmv = None
            for (o, w) in mm_groups(TOK):
                t.matmul(ps_a[:, o:o + w], k.ONER[0:1, :], ROWB[0:1, o:o + w], start=True, stop=False)
                mmv = t.matmul(ps_a[:, o:o + w], k.ONER[0:1, :], ROWB2[0:1, o:o + w], start=False, stop=True)
            PE.inc(mmv)
        k.pe(mr_pe)
        pe_mr = PE.n

        def mr_dv2(v):
            v.wait_ge(PE.h, pe_mr)
            k.dvs(v, v.tensor_copy(MB32[:, :], ps_a[:, :]))
            k.dvs(v, v.tensor_copy(ROWB[0:1, :], k.STR[0:1, :]))
            k.dvs(v, v.tensor_sub(ROWS[0:1, :], k.STR[0:1, :], ROWB[0:1, :]))
            DV.inc(v.tensor_copy(ROWB2[0:1, :], ROWS[0:1, :]))
        k.dv(mr_dv2)
        dv_mr2 = DV.n

        def mr_pe2(t):
            t.wait_ge(DV.h, dv_mr2)
            mmv = None
            for (o, w) in mm_groups(TOK):
                t.matmul(ps_a[:, o:o + w], k.ONER[0:1, :], ROWB[0:1, o:o + w], start=True, stop=False)
                mmv = t.matmul(ps_a[:, o:o + w], k.ONER[0:1, :], ROWB2[0:1, o:o + w], start=False, stop=True)
            PE.inc(mmv)
        k.pe(mr_pe2)
        pe_mr2 = PE.n

        def mr_dv3(v):
            v.wait_ge(PE.h, pe_mr2)
            DV.inc(v.tensor_copy(RB32[:, :], ps_a[:, :]))
        k.dv(mr_dv3)
        dv_rb = DV.n

        XRD = fph.enter_context(nc.sbuf_tensor(f"f_XRD{sfx}", [128, TOK], F32))
        XWR = fph.enter_context(nc.sbuf_tensor(f"f_XWR{sfx}", [128, TOK], F32))
        CEXT = fph.enter_context(nc.sbuf_tensor(f"f_CEXT{sfx}", [128, 4 * 128], BF16))
        wr_done = {}
        for dc in range(NCH):
            def o_ld(e, dc=dc):
                if dc == 0:
                    e.wait_ge(CC.h, cc_cex)
                    e.wait_ge(DV.h, dv_rb)
                PE.wait(e)
                if dc >= 1:
                    e.wait_ge(DV.h, wr_done[dc - 1])
                    LD2.wait(e)
                LD.dma(e.dma_start(out=XRD[:, :], in_=io["xout"][dc, :, :]))
                for jc in range(4):
                    LD.dma(e.dma_start(out=CEXT[:, jc * 128:(jc + 1) * 128],
                                       in_=io["cex_all"][jc * 128:(jc + 1) * 128,
                                                         dc * 128:(dc + 1) * 128]))
            k.sync(o_ld)
            ld_o = LD.n

            def sc_pe(t, dc=dc):
                t.wait_ge(LD.h, ld_o)
                t.wait_ge(DV.h, dv_prw)
                DV.wait(t)
                mmv = None
                for (o, w) in mm_groups(TOK):
                    for jc in range(4):
                        mmv = t.matmul(ps_a[:, o:o + w], CEXT[:, jc * 128:(jc + 1) * 128],
                                       PRW[:, jc * TOK + o: jc * TOK + o + w],
                                       start=(jc == 0), stop=(jc == 3))
                PE.inc(mmv)
            k.pe(sc_pe)
            pe_sc13 = PE.n

            def o_dv(v, dc=dc):
                v.wait_ge(PE.h, pe_sc13)
                xr = XRD[:, :]
                xw = XWR[:, :]
                k.dvs(v, v.tensor_sub(xw, xr, MB32[:, :]))
                k.dvs(v, v.tensor_mul(xw, xw, RB32[:, :]))
                k.dvs(v, v.tensor_add(xw, xw, xr))
                DV.inc(v.tensor_add(xw, xw, ps_a[:, :]))
                wr_done[dc] = DV.n
            k.dv(o_dv)
            dv_o = DV.n

            def o_st(e, dc=dc):
                e.wait_ge(DV.h, dv_o)
                LD2.dma(e.dma_start(out=io["xout"][dc, :, :], in_=XWR[:, :]))
            k.sync(o_st)
    fph.close()



G2RC_B = [(0, 0), (0, 512), (1, 0), (1, 512), (1, 1024), (1, 1536), (0, 1024), (0, 1536)]


def attn_b_export(k, io):
    LD2, CC = k.sems["LD2"], k.sems["CC"]

    def xb_st(e):
        k.fence(e)
        for dc in range(NCH):
            LD2.dma(e.dma_start(out=io["xb_own_d"][dc * 128:(dc + 1) * 128, :],
                                in_=k.XLN[:, dc * TOK:(dc + 1) * TOK]))
    k.sync(xb_st)

    def xb_cc(g):
        LD2.wait(g)
        CC.inc(g.collective_compute(
            "AllGather", ALU.bypass,
            replica_groups=[[0, 1], [2, 3], [4, 5], [6, 7]],
            ins=[io["xb_own_d"][:, :]], outs=[io["xb_all_d"][:, :]]), 1)
    k.gp(xb_cc)
    k._cc_xb = CC.n


def attn_b_main(k, io):
    nc = k.nc
    LD, LD2, GP, CC, PE, DV, AC = (k.sems[n] for n in ("LD", "LD2", "GP", "CC", "PE", "DV", "AC"))
    LA, LB, LC, LDD, LE, SA, SB = (k.sems[n] for n in ("LA", "LB", "LC", "LDD", "LE", "SA", "SB"))
    SCB = 1.0 / float(np.sqrt(128.0))
    cc_xb = k._cc_xb
    dv0, ac0, pe0 = DV.n, AC.n, PE.n

    qb8_d = nc.dram_tensor("qb8_d", [1024, S], BF16).ap()
    kb8_d = nc.dram_tensor("kb8_d", [1024, S], BF16).ap()
    vb8_d = nc.dram_tensor("vb8_d", [1024, S], BF16).ap()
    dpart_d = nc.dram_tensor("dpart_d", [2 * D, TOK], F32).ap()
    dred_d = nc.dram_tensor("dred_d", [D, TOK], F32).ap()

    with contextlib.ExitStack() as ph:
        sbb = lambda n, s, d: ph.enter_context(nc.sbuf_tensor(n, list(s), d))

        with contextlib.ExitStack() as pq:
            sbq = lambda n, s, d: pq.enter_context(nc.sbuf_tensor(n, list(s), d))
            WQ8 = sbq("b_WQ8", [128, 8 * D], BF16)
            WK8 = sbq("b_WK8", [128, 8 * D], BF16)
            WV8 = sbq("b_WV8", [128, 8 * D], BF16)
            WSTG = sbq("b_WSTG", [128, 2 * D], F32)
            XFG = sbq("b_XFG", [128, 2 * NCH * 512], BF16)
            QSTG = sbq("b_QSTG", [128, 2 * 512], BF16)
            BB = sbq("b_BB", [128, 24], F32)
            ps_q = pq.enter_context(nc.psum_tensor("b_psq", [128, 2 * 512], F32))
            WALL = (WQ8, WK8, WV8)
            WSRC = (io["wqb8"], io["wkb8"], io["wvb8"])
            QKV_D = (qb8_d, kb8_d, vb8_d)

            ld_ws, dv_cast = {}, {}
            for wi in range(24):
                sw = (LA, LB)[wi % 2]

                def wc_ld(e, wi=wi, sw=sw):
                    if wi == 0:
                        e.wait_ge(DV.h, dv0)
                        e.wait_ge(AC.h, ac0)
                        e.wait_ge(PE.h, pe0)
                        sw.dma(e.dma_start(out=BB[:, 0:8], in_=io["bqb8"][:, :]))
                        sw.dma(e.dma_start(out=BB[:, 8:16], in_=io["bkb8"][:, :]))
                        sw.dma(e.dma_start(out=BB[:, 16:24], in_=io["bvb8"][:, :]))
                    if wi >= 2:
                        e.wait_ge(DV.h, dv_cast[wi - 2])
                    sw.dma(e.dma_start(out=WSTG[:, (wi % 2) * D:(wi % 2 + 1) * D],
                                       in_=WSRC[wi // 8][wi % 8, :, :]))
                    ld_ws[wi] = sw.n
                k.sync(wc_ld)

                def wc_dv(v, wi=wi, sw=sw):
                    v.wait_ge(sw.h, ld_ws[wi])
                    DV.inc(v.tensor_copy(WALL[wi // 8][:, (wi % 8) * D:(wi % 8 + 1) * D],
                                         WSTG[:, (wi % 2) * D:(wi % 2 + 1) * D]))
                    dv_cast[wi] = DV.n
                k.dv(wc_dv)
            dv_wcast = DV.n

            ld_g, pe_g, dv_ev, st_ev = {}, {}, {}, {}
            for gidx in range(8):
                rnk, col = G2RC_B[gidx]
                sx = (LC, LDD)[gidx % 2]

                def x_ld(e, gidx=gidx, rnk=rnk, col=col, sx=sx):
                    if gidx == 0:
                        e.wait_ge(DV.h, dv0)
                        e.wait_ge(AC.h, ac0)
                        e.wait_ge(PE.h, pe0)
                    e.wait_ge(CC.h, cc_xb)
                    if gidx >= 2:
                        e.wait_ge(PE.h, pe_g[gidx - 2])
                    base = (gidx % 2) * NCH * 512
                    for dc in range(NCH):
                        sx.dma(e.dma_start(
                            out=XFG[:, base + dc * 512: base + (dc + 1) * 512],
                            in_=io["xb_all_d"][rnk * D + dc * 128: rnk * D + (dc + 1) * 128,
                                               col:col + 512]))
                    ld_g[gidx] = sx.n
                k.sync(x_ld)

                for wi in range(24):
                    mi, m = wi // 8, wi % 8
                    gi = gidx * 24 + wi

                    def qkv_pe(t, gidx=gidx, wi=wi, mi=mi, m=m, gi=gi, sx=sx):
                        t.wait_ge(sx.h, ld_g[gidx])
                        if gi == 0:
                            t.wait_ge(DV.h, dv_wcast)
                        if gi >= 2:
                            t.wait_ge(DV.h, dv_ev[gi - 2])
                        base = (gidx % 2) * NCH * 512
                        W = WALL[mi]
                        mmv = None
                        for dc in range(NCH):
                            mmv = t.matmul(ps_q[:, (gi % 2) * 512:(gi % 2 + 1) * 512],
                                           W[:, m * D + dc * 128: m * D + (dc + 1) * 128],
                                           XFG[:, base + dc * 512: base + (dc + 1) * 512],
                                           start=(dc == 0), stop=(dc == NCH - 1))
                        PE.inc(mmv)
                        pe_g[gidx] = PE.n
                    k.pe(qkv_pe)
                    pe_m = PE.n

                    def qkv_dv(v, mi=mi, m=m, gi=gi, pe_m=pe_m):
                        v.wait_ge(PE.h, pe_m)
                        if gi >= 2:
                            v.wait_ge((SA, SB)[gi % 2].h, st_ev[gi - 2])
                        DV.inc(v.tensor_scalar(QSTG[:, (gi % 2) * 512:(gi % 2 + 1) * 512],
                                               ps_q[:, (gi % 2) * 512:(gi % 2 + 1) * 512],
                                               BB[:, 8 * mi + m: 8 * mi + m + 1], None, ALU.add))
                        dv_ev[gi] = DV.n
                    k.dv(qkv_dv)

                    def qkv_st(e, gidx=gidx, mi=mi, m=m, gi=gi):
                        e.wait_ge(DV.h, dv_ev[gi])
                        (SA, SB)[gi % 2].dma(e.dma_start(
                            out=QKV_D[mi][m * 128:(m + 1) * 128, gidx * 512:(gidx + 1) * 512],
                            in_=QSTG[:, (gi % 2) * 512:(gi % 2 + 1) * 512]))
                        st_ev[gi] = (SA, SB)[gi % 2].n
                    k.sync(qkv_st)
        sa_qkv, sb_qkv = SA.n, SB.n
        k.tap("qb8", qb8_d[:, :], [1024, S], BF16)
        k.tap("kb8", kb8_d[:, :], [1024, S], BF16)
        k.tap("vb8", vb8_d[:, :], [1024, S], BF16)

        KH = sbb("b_KH", [128, S], BF16)
        QH = sbb("b_QH", [128, S], BF16)
        VT = sbb("b_VT", [128, S], BF16)
        SQS = sbb("b_SQS", [128, S], BF16)
        EXPP = sbb("b_EXPP", [128, 2 * 512], BF16)
        SMSB = sbb("b_SMSB", [128, 2 * 512], F32)
        DEN = sbb("b_DEN", [128, 512], F32)
        OAVB = sbb("b_OAVB", [128, 8 * S], BF16)
        TRIM = sbb("b_TRIM", [128, 4 * 512], BF16)
        MXB = sbb("b_MXB", [1, 8], F32)
        NCB = sbb("b_NCB", [128, 1], F32)
        NCBB = sbb("b_NCBB", [1, 4], BF16)
        RDH = sbb("b_RDH", [1, 512], BF16)
        RDL = sbb("b_RDL", [1, 512], BF16)
        WPSF = sbb("b_WPSF", [128, 2 * 1024], F32)
        WPS = sbb("b_WPS", [128, 2 * 1024], BF16)
        DSTB = sbb("b_DSTB", [128, 2 * 512], F32)
        XRD2 = sbb("b_XRD", [128, 2 * TOK], F32)
        DRD2 = sbb("b_DRD", [128, 2 * TOK], F32)
        XWR2 = sbb("b_XWR", [128, 2 * TOK], F32)
        BPM = sbb("b_BPM", [128, NCH], F32)
        ps_s = ph.enter_context(nc.psum_tensor("b_pss", [128, 2 * 512], F32))
        ps_av = ph.enter_context(nc.psum_tensor("b_psav", [128, 512], F32))
        ps_r = ph.enter_context(nc.psum_tensor("b_psr", [128, 512], F32))
        ps_t2 = ph.enter_context(nc.psum_tensor("b_pst2", [128, 256], BF16))

        def tri_ld(e):
            e.wait_ge(PE.h, PE.n)
            LE.dma(e.dma_start(out=TRIM[:, :], in_=io["trimask"][:, :]))
            LE.dma(e.dma_start(out=BPM[:, :], in_=io["bpm"][:, :]))
        k.sync(tri_ld)
        ld_bpm = LE.n

        pe_head_end = 0
        ac_head_end = AC.n
        dv_pq = DV.n
        pe_pq = PE.n
        for hh in range(8):
            sh = (LA, LB)[hh % 2]

            def h_ld(e, hh=hh, sh=sh, pe_head_end=pe_head_end, ac_head_end=ac_head_end):
                e.wait_ge(PE.h, pe_head_end)
                e.wait_ge(AC.h, ac_head_end)
                if hh == 0:
                    e.wait_ge(SA.h, sa_qkv)
                    e.wait_ge(SB.h, sb_qkv)
                sh.dma(e.dma_start(out=KH[:, :], in_=kb8_d[hh * 128:(hh + 1) * 128, :]))
                sh.dma(e.dma_start(out=QH[:, :], in_=qb8_d[hh * 128:(hh + 1) * 128, :]))
                for kc2 in range(16):
                    sh.dma(e.dma_start(out=SQS[:, kc2 * 256:(kc2 + 1) * 256],
                                       in_=vb8_d[hh * 128:(hh + 1) * 128,
                                                 kc2 * 256:(kc2 + 1) * 256]))
            k.sync(h_ld)
            ld_h_sem, ld_h = sh, sh.n

            for kc2 in range(16):
                def v_pe(t, kc2=kc2, ld_h_sem=ld_h_sem, ld_h=ld_h):
                    t.wait_ge(ld_h_sem.h, ld_h)
                    DV.wait(t)
                    t.transpose(ps_t2[:, 0:128], SQS[:, kc2 * 256:kc2 * 256 + 128],
                                k.IDENT[:, :])
                    PE.inc(t.transpose(ps_t2[:, 128:256],
                                       SQS[:, kc2 * 256 + 128:(kc2 + 1) * 256], k.IDENT[:, :]))
                k.pe(v_pe)
                pe_v2 = PE.n

                def v_dv(v, kc2=kc2, pe_v2=pe_v2):
                    v.wait_ge(PE.h, pe_v2)
                    v.tensor_copy(VT[:, kc2 * 256:kc2 * 256 + 128], ps_t2[:, 0:128])
                    DV.inc(v.tensor_copy(VT[:, kc2 * 256 + 128:(kc2 + 1) * 256],
                                         ps_t2[:, 128:256]))
                k.dv(v_dv)
            pe_vlast = PE.n

            def nb_ac(s, hh=hh, ld_h_sem=ld_h_sem, ld_h=ld_h, pe_vlast=pe_vlast):
                s.wait_ge(ld_h_sem.h, ld_h)
                s.wait_ge(PE.h, pe_vlast)
                AC.inc(s.activation(SQS[:, :], KH[:, :], AF.Square))
            k.ac(nb_ac)
            ac_nb = AC.n

            for j, (o, w) in enumerate(mm_groups(S)):
                def nb_pe(t, o=o, w=w, ac_nb=ac_nb):
                    t.wait_ge(AC.h, ac_nb)
                    DV.wait(t)
                    PE.inc(t.matmul(ps_r[0:1, 0:w], k.ONEC[:, :], SQS[:, o:o + w],
                                    start=True, stop=True))
                k.pe(nb_pe)
                pe_nb = PE.n

                def nb_dv(v, j=j, w=w, pe_nb=pe_nb):
                    v.wait_ge(PE.h, pe_nb)
                    DV.wait(v)
                    k.dvs(v, v.reduce_max(MXB[0:1, 1:2], ps_r[0:1, 0:w], axis=AX.X))
                    if j == 0:
                        DV.inc(v.tensor_copy(MXB[0:1, 0:1], MXB[0:1, 1:2]))
                    else:
                        k.dvs(v, v.tensor_tensor(MXB[0:1, 2:3], MXB[0:1, 0:1], MXB[0:1, 1:2],
                                                 ALU.max))
                        DV.inc(v.tensor_copy(MXB[0:1, 0:1], MXB[0:1, 2:3]))
                k.dv(nb_dv)
            pe_nb_all = PE.n

            def nq_ac(s, hh=hh, pe_nb_all=pe_nb_all):
                s.wait_ge(PE.h, pe_nb_all)
                AC.inc(s.activation(SQS[:, :], QH[:, :], AF.Square))
            k.ac(nq_ac)
            ac_nq = AC.n

            for j, (o, w) in enumerate(mm_groups(S)):
                def nq_pe(t, o=o, w=w, ac_nq=ac_nq):
                    t.wait_ge(AC.h, ac_nq)
                    DV.wait(t)
                    PE.inc(t.matmul(ps_r[0:1, 0:w], k.ONEC[:, :], SQS[:, o:o + w],
                                    start=True, stop=True))
                k.pe(nq_pe)
                pe_nq = PE.n

                def nq_dv(v, j=j, w=w, pe_nq=pe_nq):
                    v.wait_ge(PE.h, pe_nq)
                    DV.wait(v)
                    k.dvs(v, v.reduce_max(MXB[0:1, 3:4], ps_r[0:1, 0:w], axis=AX.X))
                    if j == 0:
                        DV.inc(v.tensor_copy(MXB[0:1, 4:5], MXB[0:1, 3:4]))
                    else:
                        k.dvs(v, v.tensor_tensor(MXB[0:1, 5:6], MXB[0:1, 4:5], MXB[0:1, 3:4],
                                                 ALU.max))
                        DV.inc(v.tensor_copy(MXB[0:1, 4:5], MXB[0:1, 5:6]))
                k.dv(nq_dv)

            def cb_dv(v):
                DV.wait(v)
                DV.inc(v.tensor_mul(MXB[0:1, 6:7], MXB[0:1, 0:1], MXB[0:1, 4:5]))
            k.dv(cb_dv)

            def cb_ac(s):
                DV.wait(s)
                AC.inc(s.activation(MXB[0:1, 2:3], MXB[0:1, 6:7], AF.Sqrt, scale=1.0 / 128.0))
            k.ac(cb_ac)
            ac_cb = AC.n

            def cb_dv2(v, ac_cb=ac_cb):
                v.wait_ge(AC.h, ac_cb)
                k.dvs(v, v.tensor_scalar(MXB[0:1, 7:8], MXB[0:1, 2:3], -1.0, -1.0,
                                         ALU.mult, ALU.add))
                DV.inc(v.tensor_copy(NCBB[0:1, 0:1], MXB[0:1, 7:8]))
            k.dv(cb_dv2)
            dv_ncb = DV.n

            def cb_pe(t, dv_ncb=dv_ncb):
                t.wait_ge(DV.h, dv_ncb)
                PE.inc(t.matmul(ps_r[:, 0:1], k.ONER[0:1, :], NCBB[0:1, 0:1],
                                start=True, stop=True))
            k.pe(cb_pe)
            pe_ncb = PE.n

            def cb_dv3(v, pe_ncb=pe_ncb):
                v.wait_ge(PE.h, pe_ncb)
                AC.wait(v)
                DV.inc(v.tensor_copy(NCB[:, :], ps_r[:, 0:1]))
            k.dv(cb_dv3)
            dv_ncbf = DV.n

            for qs in range(8):
                nkc = 4 * qs + 4

                def ab_init(v, dv_ncbf=dv_ncbf):
                    v.wait_ge(DV.h, dv_ncbf)
                    PE.wait(v)
                    DV.inc(v.memset(DEN[:, :], 0.0))
                k.dv(ab_init)

                ac_e, dv_d, pe_a = {}, {}, {}
                for kc in range(nkc):
                    par = kc % 2
                    diag = kc - 4 * qs

                    def sb_pe(t, qs=qs, kc=kc, par=par, ld_h_sem=ld_h_sem, ld_h=ld_h):
                        t.wait_ge(ld_h_sem.h, ld_h)
                        if kc < 2:
                            DV.wait(t)
                        else:
                            t.wait_ge(AC.h, ac_e[kc - 2])
                        PE.inc(t.matmul(ps_s[:, par * 512:(par + 1) * 512],
                                        KH[:, kc * 128:(kc + 1) * 128],
                                        QH[:, qs * 512:(qs + 1) * 512],
                                        start=True, stop=True))
                    k.pe(sb_pe)
                    pe_sb = PE.n

                    if diag >= 0:
                        def msk_dv(v, par=par, diag=diag, pe_sb=pe_sb, kc=kc):
                            v.wait_ge(PE.h, pe_sb)
                            if kc >= 2:
                                v.wait_ge(AC.h, ac_e[kc - 2])
                            DV.inc(v.tensor_add(SMSB[:, par * 512:(par + 1) * 512],
                                                ps_s[:, par * 512:(par + 1) * 512],
                                                TRIM[:, diag * 512:(diag + 1) * 512]))
                        k.dv(msk_dv)
                        dv_msk = DV.n
                    else:
                        dv_msk = None

                    def eb_ac(s, kc=kc, par=par, dv_msk=dv_msk, pe_sb=pe_sb,
                              dv_pq=dv_pq, pe_pq=pe_pq):
                        s.wait_ge(PE.h, pe_sb)
                        if dv_msk is not None:
                            s.wait_ge(DV.h, dv_msk)
                        if kc >= 2:
                            s.wait_ge(DV.h, dv_d[kc - 2])
                            s.wait_ge(PE.h, pe_a[kc - 2])
                        else:
                            s.wait_ge(DV.h, dv_pq)
                            s.wait_ge(PE.h, pe_pq)
                        src = (SMSB if dv_msk is not None else ps_s)
                        AC.inc(s.activation(EXPP[:, par * 512:(par + 1) * 512],
                                            src[:, par * 512:(par + 1) * 512], AF.Exp,
                                            bias=NCB[:, 0:1], scale=SCB))
                        ac_e[kc] = AC.n
                    k.ac(eb_ac)

                    def db_dv(v, kc=kc, par=par):
                        v.wait_ge(AC.h, ac_e[kc])
                        DV.inc(v.tensor_add(DEN[:, :], DEN[:, :],
                                            EXPP[:, par * 512:(par + 1) * 512]))
                        dv_d[kc] = DV.n
                    k.dv(db_dv)

                    def avb_pe(t, kc=kc, par=par, nkc=nkc):
                        t.wait_ge(AC.h, ac_e[kc])
                        PE.inc(t.matmul(ps_av[:, 0:512], VT[:, kc * 128:(kc + 1) * 128],
                                        EXPP[:, par * 512:(par + 1) * 512],
                                        start=(kc == 0), stop=(kc == nkc - 1)))
                        pe_a[kc] = PE.n
                    k.pe(avb_pe)

                def dnb_pe(t, nkc=nkc):
                    t.wait_ge(DV.h, dv_d[nkc - 1])
                    PE.inc(t.matmul(ps_r[0:1, 0:512], k.ONEC32[:, :], DEN[:, :],
                                    start=True, stop=True))
                k.pe(dnb_pe)
                pe_dnb = PE.n

                def rcb_dv(v, qs=qs, hh=hh, nkc=nkc, pe_dnb=pe_dnb):
                    v.wait_ge(PE.h, pe_dnb)
                    k.dvs(v, v.reciprocal(DEN[0:1, 0:512], ps_r[0:1, 0:512]))
                    k.dvs(v, v.tensor_copy(RDH[0:1, :], DEN[0:1, 0:512]))
                    k.dvs(v, v.tensor_sub(DEN[0:1, 0:512], DEN[0:1, 0:512], RDH[0:1, :]))
                    k.dvs(v, v.tensor_copy(RDL[0:1, :], DEN[0:1, 0:512]))
                    v.wait_ge(PE.h, pe_a[nkc - 1])
                    DV.inc(v.tensor_copy(EXPP[:, 0:512], ps_av[:, 0:512]))
                k.dv(rcb_dv)
                dv_rcb = DV.n

                def bcb_pe(t, nkc=nkc, dv_rcb=dv_rcb):
                    t.wait_ge(DV.h, dv_rcb)
                    t.wait_ge(AC.h, ac_e[nkc - 1])
                    t.matmul(ps_s[:, 0:512], k.ONER[0:1, :], RDH[0:1, :],
                             start=True, stop=False)
                    PE.inc(t.matmul(ps_s[:, 0:512], k.ONER[0:1, :], RDL[0:1, :],
                                    start=False, stop=True))
                k.pe(bcb_pe)
                pe_bcb = PE.n

                def nmb_dv(v, qs=qs, hh=hh, pe_bcb=pe_bcb):
                    v.wait_ge(PE.h, pe_bcb)
                    sl = slice(hh * S + qs * 512, hh * S + (qs + 1) * 512)
                    DV.inc(v.tensor_mul(OAVB[:, sl], EXPP[:, 0:512], ps_s[:, 0:512]))
                k.dv(nmb_dv)
                dv_pq = DV.n
                pe_pq = PE.n

            pe_head_end = PE.n
            ac_head_end = AC.n
        dv_att_end = DV.n
        k.tap("oavb", OAVB[:, :], [128, 8 * S], BF16)

        ld_wp, dv_wp, pe_wp, dv_dst, st_d = {}, {}, {}, {}, {}
        for dc in range(NCH):
            swp = (LC, LDD)[dc % 2]

            def wp_ld(e, dc=dc, swp=swp):
                if dc >= 2:
                    e.wait_ge(DV.h, dv_wp[dc - 2])
                for m in range(8):
                    swp.dma(e.dma_start(out=WPSF[:, (dc % 2) * 1024 + m * 128:
                                               (dc % 2) * 1024 + (m + 1) * 128],
                                        in_=io["wpb8"][m, :, dc * 128:(dc + 1) * 128]))
                ld_wp[dc] = swp.n
            k.sync(wp_ld)

            def wp_dv(v, dc=dc, swp=swp):
                v.wait_ge(swp.h, ld_wp[dc])
                if dc >= 2:
                    v.wait_ge(PE.h, pe_wp[dc - 2])
                DV.inc(v.tensor_copy(WPS[:, (dc % 2) * 1024:(dc % 2 + 1) * 1024],
                                     WPSF[:, (dc % 2) * 1024:(dc % 2 + 1) * 1024]))
                dv_wp[dc] = DV.n
            k.dv(wp_dv)

            for g in range(8):
                gg = dc * 8 + g

                def wp_pe(t, dc=dc, g=g, gg=gg):
                    t.wait_ge(DV.h, dv_wp[dc])
                    if gg == 0:
                        t.wait_ge(DV.h, dv_att_end)
                    if gg >= 2:
                        t.wait_ge(DV.h, dv_dst[gg - 2])
                    mmv = None
                    for m in range(8):
                        mmv = t.matmul(ps_s[:, (g % 2) * 512:(g % 2 + 1) * 512],
                                       WPS[:, (dc % 2) * 1024 + m * 128:
                                           (dc % 2) * 1024 + (m + 1) * 128],
                                       OAVB[:, m * S + g * 512: m * S + (g + 1) * 512],
                                       start=(m == 0), stop=(m == 7))
                    PE.inc(mmv)
                    pe_wp[dc] = PE.n
                k.pe(wp_pe)
                pe_g2 = PE.n

                def wp_dve(v, g=g, gg=gg, pe_g2=pe_g2):
                    v.wait_ge(PE.h, pe_g2)
                    if gg >= 2:
                        v.wait_ge((SA, SB)[gg % 2].h, st_d[gg - 2])
                    DV.inc(v.tensor_copy(DSTB[:, (g % 2) * 512:(g % 2 + 1) * 512],
                                         ps_s[:, (g % 2) * 512:(g % 2 + 1) * 512]))
                    dv_dst[gg] = DV.n
                k.dv(wp_dve)

                def wp_st(e, dc=dc, g=g, gg=gg):
                    e.wait_ge(DV.h, dv_dst[gg])
                    rnk, col = G2RC_B[g]
                    (SA, SB)[gg % 2].dma(e.dma_start(
                        out=dpart_d[rnk * D + dc * 128: rnk * D + (dc + 1) * 128,
                                    col:col + 512],
                        in_=DSTB[:, (g % 2) * 512:(g % 2 + 1) * 512]))
                    st_d[gg] = (SA, SB)[gg % 2].n
                k.sync(wp_st)

        k.tap("dpart", dpart_d[:, :], [2 * D, TOK], F32)

        def rs_gp(g):
            LD2.wait(g)
            SA.wait(g)
            SB.wait(g)
            CC.inc(g.collective_compute(
                "ReduceScatter", ALU.add,
                replica_groups=[[0, 1], [2, 3], [4, 5], [6, 7]],
                ins=[dpart_d[:, :]], outs=[dred_d[:, :]]), 1)
        k.gp(rs_gp)
        cc_rs = CC.n

        ld_ox, dv_ox, st_ox = {}, {}, {}
        for dc in range(NCH):
            sox = (LA, LB)[dc % 2]

            def ox_ld(e, dc=dc, sox=sox):
                e.wait_ge(CC.h, cc_rs)
                if dc >= 2:
                    e.wait_ge(DV.h, dv_ox[dc - 2])
                sl = slice((dc % 2) * TOK, (dc % 2 + 1) * TOK)
                sox.dma(e.dma_start(out=XRD2[:, sl], in_=io["xout"][dc, :, :]))
                sox.dma(e.dma_start(out=DRD2[:, sl], in_=dred_d[dc * 128:(dc + 1) * 128, :]))
                ld_ox[dc] = sox.n
            k.sync(ox_ld)

            def ox_dv(v, dc=dc, sox=sox):
                v.wait_ge(sox.h, ld_ox[dc])
                v.wait_ge(LE.h, ld_bpm)
                if dc >= 2:
                    v.wait_ge((SA, SB)[dc % 2].h, st_ox[dc - 2])
                sl = slice((dc % 2) * TOK, (dc % 2 + 1) * TOK)
                v.tensor_add(XRD2[:, sl], XRD2[:, sl], DRD2[:, sl])
                DV.inc(v.tensor_scalar(XWR2[:, sl], XRD2[:, sl], BPM[:, dc:dc + 1],
                                       None, ALU.add))
                dv_ox[dc] = DV.n
            k.dv(ox_dv)

            def ox_st(e, dc=dc):
                e.wait_ge(DV.h, dv_ox[dc])
                (SA, SB)[dc % 2].dma(e.dma_start(out=io["xout"][dc, :, :],
                                    in_=XWR2[:, (dc % 2) * TOK:(dc % 2 + 1) * TOK]))
                st_ox[dc] = (SA, SB)[dc % 2].n
            k.sync(ox_st)


def build(taps=()):
    k = K(taps)
    nc = k.nc

    def din(name, shape, dt=F32):
        return nc.dram_tensor(name, list(shape), dt, kind="ExternalInput").ap()

    io = {}
    io["xin"] = din("xin", [NCH, 128, TOK])
    io["wq_a"] = din("wq_a", [NH_A, 128, NCH * QD_A])
    io["wkv_a"] = din("wkv_a", [128, NCH * LORA])
    io["wk_up_a"] = din("wk_up_a", [NH_A, 128, 4 * QD_A])
    io["wv_up_a"] = din("wv_up_a", [NH_A, 128, 4 * VD_A])
    io["wo_a"] = din("wo_a", [4, 128, 4 * D])
    io["router_w"] = din("router_w", [128, NCH])
    io["ln1w"] = din("ln1w", [128, NCH])
    io["ln2w"] = din("ln2w", [128, NCH])
    io["wq_f"] = din("wq_f", [16, 128, NCH * 128])
    io["wk_f"] = din("wk_f", [8, 128, NCH * 128])
    io["wv_f"] = din("wv_f", [8, 128, NCH * 128])
    io["bq_f"] = din("bq_f", [128, 16])
    io["bk_f"] = din("bk_f", [128, 8])
    io["bv_f"] = din("bv_f", [128, 8])
    io["wo_f"] = din("wo_f", [16, 128, D])
    io["wg_f"] = din("wg_f", [NICH, 128, NCH * 128])
    io["wu_f"] = din("wu_f", [NICH, 128, NCH * 128])
    io["wd_f"] = din("wd_f", [NICH, 128, D])
    io["wqb8"] = din("wqb8", [8, 128, NCH * 128])
    io["wkb8"] = din("wkb8", [8, 128, NCH * 128])
    io["wvb8"] = din("wvb8", [8, 128, NCH * 128])
    io["bqb8"] = din("bqb8", [128, 8])
    io["bkb8"] = din("bkb8", [128, 8])
    io["bvb8"] = din("bvb8", [128, 8])
    io["wpb8"] = din("wpb8", [8, 128, D])
    io["bpm"] = din("bpm", [128, NCH])
    io["ropeq_cs"] = din("ropeq_cs", [64, TOK], BF16)
    io["ropeq_sc"] = din("ropeq_sc", [64, TOK], BF16)
    io["ropek_cs"] = din("ropek_cs", [64, S], BF16)
    io["ropek_sc"] = din("ropek_sc", [64, S], BF16)
    io["trimask"] = din("trimask", [128, 4 * 512], BF16)
    io["st_idx"] = din("st_idx", [128, 32])
    io["triw"] = din("triw", [128, 128], BF16)
    io["qflat"] = din("qflat", [1, OWNSEL])
    io["jown"] = din("jown", [128, OWNSEL])
    io["ownpos"] = din("ownpos", [1, TOK])
    io["cos_f_tbl"] = din("cos_f_tbl", [S, 256], BF16)
    io["xout"] = nc.dram_tensor("xout", [NCH, 128, TOK], F32, kind="ExternalOutput").ap()

    def dsc(name, shape, dt):
        return nc.dram_tensor(name, list(shape), dt).ap()

    io["ckv_own_d"] = dsc("ckv_own_d", [4 * 128, TOK], BF16)
    io["ckv_all_d"] = dsc("ckv_all_d", [8 * 128, TOK], BF16)
    io["sc_own_d"] = dsc("sc_own_d", [1, TOK], F32)
    io["sc_all_d"] = dsc("sc_all_d", [2, TOK], F32)
    io["xln_tm_own"] = dsc("xln_tm_own", [TOK, D], BF16)
    io["xln_tm_all"] = dsc("xln_tm_all", [2 * TOK, D], BF16)
    io["kf_own"] = dsc("kf_own", [1024, 256], BF16)
    io["kf_all"] = dsc("kf_all", [8192, 256], BF16)
    io["vtm_own"] = dsc("vtm_own", [256, 1024], BF16)
    io["vtm_all"] = dsc("vtm_all", [2048, 1024], BF16)
    io["cex_own"] = dsc("cex_own", [256, D], BF16)
    io["cex_all"] = dsc("cex_all", [512, D], BF16)
    io["xb_own_d"] = dsc("xb_own_d", [D, TOK], BF16)
    io["xb_all_d"] = dsc("xb_all_d", [2 * D, TOK], BF16)

    with k.ctx:
        k.ONEC = k.sb("ONEC", [128, 1], BF16)
        k.ONEC32 = k.sb("ONEC32", [128, 1], F32)
        k.ONER = k.sb("ONER", [1, 128], BF16)
        k.IDENT = k.sb("IDENT", [128, 128], BF16)
        k.IDENT32 = k.sb("IDENT32", [128, 128], F32)

        k.sems = {n: Sem(k.ctx, nc, n) for n in
                  ("LD", "LD2", "GP", "CC", "PE", "DV", "AC", "ST",
                   "LA", "LB", "LC", "LDD", "LE", "SA", "SB")}
        GP, DV = k.sems["GP"], k.sems["DV"]

        k.blk = k.ctx.enter_context(nc.Block())

        with nc.sbuf_tensor("iotc", [128, 128], F32) as iotc, \
             nc.sbuf_tensor("iotp", [128, 1], F32) as iotp:
            def ig(g):
                g.memset(k.ONEC[:, :], 1.0)
                g.memset(k.ONEC32[:, :], 1.0)
                g.memset(k.ONER[:, :], 1.0)
                g.iota(iotc[:, :], pattern=[[1, 128]], base=0, channel_multiplier=0,
                       allow_small_or_imprecise_dtypes=True)
                GP.inc(g.iota(iotp[:, :], pattern=[[0, 1]], base=0, channel_multiplier=1,
                              allow_small_or_imprecise_dtypes=True), 1)
            k.gp(ig)

            def iv(v):
                GP.wait(v)
                v.tensor_single_scalar(k.IDENT[:, :], iotc[:, :], iotp[:, :], ALU.is_equal)
                DV.inc(v.tensor_single_scalar(k.IDENT32[:, :], iotc[:, :], iotp[:, :], ALU.is_equal))
            k.dv(iv)

        k.tap("ident32", k.IDENT32[:, :], [128, 128], F32)
        k.tap("ident", k.IDENT[:, :], [128, 128], BF16)
        k.cast_done = {}
        precast(k, io, ["wkv_a", "wq_a", "wk_up_a", "wv_up_a", "wo_a"])
        with contextlib.ExitStack() as s0:
            alloc_stats(k, s0, scores=False)
            ln_phase(k, io, src=io["xin"], copy_to=io["xout"], exact_scores=False)
            k.tap("xln1", k.XLN[:, :], [128, NCH * TOK], BF16)
            attn_a_ab(k, io)
        attn_a_c(k, io)

        with contextlib.ExitStack() as s2:
            s2x = contextlib.ExitStack()
            alloc_stats(k, s2, scores=True, xstack=s2x)
            GPs = k.sems["GP"]
            ln_phase(k, io, src=io["xout"], copy_to=None, exact_scores=True,
                     wait_first=[(GPs, GPs.n)])
            k.tap("scro", k.SCRO[:, :], [1, TOK], F32)
            k.tap("stm2", k.STM[:, :], [1, TOK], F32)
            k.tap("str2", k.STR[:, :], [1, TOK], F32)
            k.tap("rsum1", k._dbg_rsum1[:, :], [1, 1], F32)
            k.tap("raws", k.STROW2[:, :], [1, TOK], F32)
            ffn_tm_export(k, io, 0)
            s2x.close()
            with contextlib.ExitStack() as s2r:
                k._r = ffn_router_alloc(k, s2, s2r, sfx="_0")
                ffn_router(k, io, call=0)
                r = k._r
                k.tap("sel", r.SEL[:, :], [128, 32], F32)
                k.tap("idxs", r.IDXS[:, :], [128, 12], F32)
                k.tap("owni", r.OWNI[:, :], [128, 6], F32)
            k.tap("idxi", r.IDXI[:, :], [128, 8], I32)
            k.tap("tmown", io["xln_tm_own"][:, :], [TOK, D], BF16)
            ffn_body(k, io, 0, "_f0")

        if PHASES >= 3:
            p3_p4(k, io)

        def fin(g):
            for s in k.sems.values():
                g.wait_ge(s.h, s.n)
        k.gp(fin)

    return nc, k.tap_t


def p3_p4(k, io):
    nc = k.nc
    if True:
        with contextlib.ExitStack() as s3x:
            s3 = contextlib.ExitStack()
            alloc_stats(k, s3, scores=False, xstack=s3x)
            wf3 = [(s, s.n) for s in k.sems.values()]
            ln_phase(k, io, src=io["xout"], copy_to=None, exact_scores=False,
                     wait_first=wf3)
            s3.close()
            k.tap("xln3", k.XLN[:, :], [128, NCH * TOK], BF16)
            attn_b_export(k, io)
        attn_b_main(k, io)

        if PHASES < 4:
            return
        with contextlib.ExitStack() as s4:
            s4x = contextlib.ExitStack()
            alloc_stats(k, s4, scores=True, xstack=s4x)
            wf4 = [(s, s.n) for s in k.sems.values()]
            ln_phase(k, io, src=io["xout"], copy_to=None, exact_scores=True,
                     wait_first=wf4)
            k.tap("scro4", k.SCRO[:, :], [1, TOK], F32)
            ffn_tm_export(k, io, 1)
            s4x.close()
            with contextlib.ExitStack() as s4r:
                k._r = ffn_router_alloc(k, s4, s4r, sfx="_1")
                ffn_router(k, io, call=1)
            ffn_body(k, io, 1, "_f1")


def kernel(**inputs):
    maps = host_prepare(inputs)
    nc, _ = build(taps=DEBUG_TAPS)
    res = run_bass_kernel_spmd(nc, maps, core_ids=list(range(8)))
    out = np.empty((B, S, D), np.float32)
    for c in range(8):
        b, h = c // 2, c % 2
        xo = np.asarray(res.results[c]["xout"])
        out[b, OWN_POS[h], :] = xo.reshape(D, TOK).T
    return out


def run_with_taps(inputs, taps):
    maps = host_prepare(inputs)
    nc, _ = build(taps=taps)
    return run_bass_kernel_spmd(nc, maps, core_ids=list(range(8)))
